# version 05
# baseline (speedup 1.0000x reference)
import sys
if '/opt/trn_rl_repo' not in sys.path:
    sys.path.insert(0, '/opt/trn_rl_repo')
import math
import numpy as np
import ml_dtypes
import concourse.bass as bass
import concourse.mybir as mybir
from concourse.bass_utils import run_bass_kernel_spmd

F32 = mybir.dt.float32
BF16 = mybir.dt.bfloat16
AF = mybir.ActivationFunctionType
ALU = mybir.AluOpType
AX = mybir.AxisListType

T = 2048
NCH = 16
D = 1024
SCALE = 64.0 ** -0.25
FEPS = 1e-6
SQM = float(math.sqrt(128.0))

ENGINES = ("tensor", "vector", "scalar", "gpsimd", "sync")
TAPS = set()
TAPROT = -1

XN = 0
VT = 16384
SN = 33280
QD = 50176
PH = 66560
WS = 87040
XS = 92160
ACOLS = 102400

H2 = VT


class Prog:
    def __init__(self):
        self.steps = {e: [] for e in ENGINES}
        self.count = {s: 0 for s in ("LD", "SQ", "AQ", "TC", "VC", "SC", "PC")}
        self.pending = {e: () for e in ENGINES}

    def step(self, eng, fn, after=(), sig=None, dma=False):
        after = tuple(e for e in after if e is not None) + self.pending[eng]
        self.pending[eng] = ()
        incs = ()
        ev = None
        if sig is not None:
            n = 16 if dma else 1
            self.count[sig] += n
            ev = (sig, self.count[sig])
            incs = ((sig, n),)
        self.steps[eng].append((fn, after, incs))
        return ev

    def barrier(self):
        evs = tuple((s, v) for s, v in self.count.items() if v > 0)
        for e in ENGINES:
            self.pending[e] = self.pending[e] + evs

    def emit_engine(self, eng_name, eng, sems):
        seen = {}
        for fn, waits, incs in self.steps[eng_name]:
            for (s, v) in waits:
                if seen.get(s, 0) < v:
                    eng.wait_ge(sems[s], v)
                    seen[s] = v
            inst = fn(eng)
            for (s, n) in incs:
                inst.then_inc(sems[s], n)


class B:
    def __init__(self):
        nc = bass.Bass("TRN2", target_bir_lowering=False)
        self.nc = nc
        self.p = Prog()

        def inp(name, shape):
            return nc.dram_tensor(name, list(shape), F32, kind="ExternalInput").ap()

        self.x_tok = inp("x_tok", (T, D))
        self.xp_tok = inp("xp_tok", (T, D))
        self.prevrow = inp("prevrow_fm", (128, 8))
        self.smask = inp("smask", (128, 1))
        self.wqp = inp("wqp", (512, 2048))
        self.wkp = inp("wkp", (512, 2048))
        self.wv = inp("wv", (D, D))
        self.wop = inp("wop", (1024, 1024))
        self.womega_t = inp("womega_t", (64, 128))
        self.w1p = inp("w1p", (4096, 1024))
        self.b1_t = inp("b1_t", (128, 32))
        self.w2 = inp("w2", (4096, D))
        self.b2row = inp("b2row", (1, D))
        self.gamma_t = inp("gamma_t", (128, 8))
        self.beta_t = inp("beta_t", (128, 8))
        self.wrtp = inp("wrtp", (1024, 1024))
        self.wk2p = inp("wk2p", (1024, 1024))
        self.wv2t = inp("wv2t", (D, D))
        self.mur_t = inp("mur_t", (128, 8))
        self.muk_t = inp("muk_t", (128, 8))
        self.utmask_in = inp("utmask", (128, 128))

        self.out = nc.dram_tensor("out", [T, D], F32, kind="ExternalOutput").ap()

        self.scrA = nc.dram_tensor("scrA", [T, D], BF16, kind="Internal").ap()
        self.rscr = nc.dram_tensor("rscr", [D, T], BF16, kind="Internal").ap()
        self.NROT = 4
        self.pscr = [nc.dram_tensor(f"pscr{i}", [T, 128], BF16, kind="Internal").ap() for i in range(self.NROT)]
        self.qscr = [nc.dram_tensor(f"qscr{i}", [T, 128], BF16, kind="Internal").ap() for i in range(self.NROT)]
        self.pscr_free = [None] * self.NROT
        self.qscr_free = [None] * self.NROT

        self.arena = nc.alloc_sbuf_tensor("arena", [128, ACOLS], BF16)
        self.A = self.arena.ap()
        self.cst = nc.alloc_sbuf_tensor("cst", [128, 516], BF16)
        self.C = self.cst.ap()
        self.cst2 = nc.alloc_sbuf_tensor("cst2", [128, 2120], BF16)
        self.C2 = self.cst2.ap()
        self.cstf = nc.alloc_sbuf_tensor("cstf", [128, 480], F32)
        self.CF = self.cstf.ap()

        self.pbig = nc.alloc_psum_tensor("pbig", [128, 2048], F32)
        self.PBG = self.pbig.ap()
        self.P4 = nc.alloc_psum_tensor("ps4", [128, 512], F32).ap()
        self.P5 = nc.alloc_psum_tensor("ps5", [128, 512], F32).ap()
        self.P6 = nc.alloc_psum_tensor("ps6", [128, 512], F32).ap()
        self.P7 = nc.alloc_psum_tensor("ps7", [128, 512], F32).ap()

        self.phi_pbig = [None, None]
        self.phi_stats_user = None
        self.ws_user = [None, None, None]
        self.qtk_user = None
        self.pM_user = [None, None]
        self.ptk_user = [None, None]
        self.stats_user = [None, None]
        self.kk_bank_user = {}
        self.kk_tmp_user = None
        self.r_store = None

    def tap(self, tag, ap, after):
        if tag not in TAPS:
            return
        t = self.nc.dram_tensor("dbg_" + tag, list(ap.shape), ap.dtype, kind="ExternalOutput").ap()
        self.p.step("sync", lambda s, t=t, ap=ap: s.dma_start(out=t, in_=ap), after=after, sig="SQ", dma=True)
        self.p.barrier()

    def bf(self, c0, n):
        return self.A[:, c0:c0 + n]

    def f32(self, c0, n):
        assert c0 % 2 == 0
        return self.A.bitcast(F32)[:, c0 // 2: c0 // 2 + n]

    def consts(self):
        p, C, C2, CF = self.p, self.C, self.C2, self.CF
        p.step("gpsimd", lambda g: g.dma_start(out=C[:, 0:128], in_=self.utmask_in), sig="LD", dma=True)
        p.step("gpsimd", lambda g: g.dma_start(out=C[0:64, 260:388], in_=self.womega_t), sig="LD", dma=True)
        p.step("gpsimd", lambda g: g.dma_start(out=C[64:128, 260:388], in_=self.womega_t), sig="LD", dma=True)
        p.step("gpsimd", lambda g: g.dma_start(out=self.bf(XS, 1024)[0:1, :], in_=self.b2row), sig="LD", dma=True)
        for (ap, c0) in ((self.gamma_t, 0), (self.beta_t, 8), (self.mur_t, 16), (self.muk_t, 24),
                         (self.b1_t, 48), (self.smask, 80), (self.prevrow, 136)):
            n = ap.shape[1]
            p.step("gpsimd", lambda g, ap=ap, c0=c0, n=n: g.dma_start(out=CF[:, c0:c0 + n], in_=ap), sig="LD", dma=True)
        e_cd = ("LD", p.count["LD"])

        p.step("vector", lambda v: v.memset(C[:, 388:516], -0.5), sig="VC")
        p.step("vector", lambda v: v.memset(CF[:, 146:147], 1e-5), sig="VC")
        p.step("vector", lambda v: v.memset(CF[:, 147:148], 1e6), sig="VC")
        p.step("vector", lambda v: v.memset(C[:, 128:129], 1.0), sig="VC")
        e_ones = p.step("vector", lambda v: v.memset(C[0:1, 129:257], 1.0), sig="VC")
        self.e_ones = e_ones
        p.step("vector", lambda v: v.tensor_scalar(CF[:, 32:40], CF[:, 16:24], -1.0, 1.0, ALU.mult, ALU.add), after=[e_cd], sig="VC")
        p.step("vector", lambda v: v.tensor_scalar(CF[:, 40:48], CF[:, 24:32], -1.0, 1.0, ALU.mult, ALU.add), sig="VC")
        for i, pb in ((0, self.P4), (1, self.P5)):
            e_mm = p.step("tensor", lambda t, i=i, pb=pb: t.matmul(
                pb[:, 0:512], C[0:1, 129:257], self.bf(XS, 1024)[0:1, 512 * i:512 * (i + 1)], start=True, stop=True),
                after=[e_cd, e_ones], sig="TC")
            p.step("vector", lambda v, i=i, pb=pb: v.tensor_copy(
                C2[:, 1056 + 512 * i:1056 + 512 * (i + 1)], pb[:, 0:512]), after=[e_mm], sig="VC")

        RW = lambda i: self.f32(PH + 16384, 64)[0:1, i:i + 1]
        e_pb = p.step("vector", lambda v: v.tensor_copy(C2[:, 2080:2088], CF[:, 136:144]), after=[e_cd], sig="VC")
        e_pq = p.step("vector", lambda v: v.tensor_tensor(C2[:, 2088:2096], C2[:, 2080:2088], C2[:, 2080:2088], ALU.mult), after=[e_pb], sig="VC")
        e_s1 = p.step("tensor", lambda t: t.matmul(self.P6[0:1, 0:8], C[:, 128:129], C2[:, 2080:2088], start=True, stop=True), after=[e_pb, e_ones], sig="TC")
        e_s2 = p.step("tensor", lambda t: t.matmul(self.P6[0:1, 8:16], C[:, 128:129], C2[:, 2088:2096], start=True, stop=True), after=[e_pq], sig="TC")
        e_r1 = p.step("vector", lambda v: v.tensor_reduce(RW(0), self.P6[0:1, 0:8], AX.X, ALU.add), after=[e_s1], sig="VC")
        e_r2 = p.step("vector", lambda v: v.tensor_reduce(RW(1), self.P6[0:1, 8:16], AX.X, ALU.add), after=[e_s2], sig="VC")
        e_m = p.step("vector", lambda v: v.tensor_scalar(RW(2), RW(0), 1.0 / 1024.0, None, ALU.mult), after=[e_r1], sig="VC")
        e_m2 = p.step("vector", lambda v: v.tensor_tensor(RW(3), RW(2), RW(2), ALU.mult), after=[e_m], sig="VC")
        e_va = p.step("vector", lambda v: v.scalar_tensor_tensor(RW(4), RW(1), 1.0 / 1024.0, RW(3), ALU.mult, ALU.subtract), after=[e_m2, e_r2], sig="VC")
        e_ln = p.step("scalar", lambda s: s.activation(RW(5), RW(4), AF.Ln, bias=CF[0:1, 146:147]), after=[e_va], sig="SC")
        e_rs = p.step("scalar", lambda s: s.activation(RW(6), RW(5), AF.Exp, scale=-0.5), after=[e_ln], sig="SC")
        bft = lambda i: C2[0:1, 2104 + i:2105 + i]
        e_c1 = p.step("vector", lambda v: v.tensor_copy(bft(0), RW(2)), after=[e_m], sig="VC")
        e_c2 = p.step("vector", lambda v: v.tensor_copy(bft(1), RW(6)), after=[e_rs], sig="VC")
        e_b1 = p.step("tensor", lambda t: t.matmul(self.P6[:, 20:21], C[0:1, 129:257], bft(0), start=True, stop=True), after=[e_c1], sig="TC")
        e_b2 = p.step("tensor", lambda t: t.matmul(self.P6[:, 21:22], C[0:1, 129:257], bft(1), start=True, stop=True), after=[e_c2], sig="TC")
        e_muc = p.step("vector", lambda v: v.tensor_copy(CF[:, 144:145], self.P6[:, 20:21]), after=[e_b1], sig="VC")
        e_rsc = p.step("vector", lambda v: v.tensor_copy(CF[:, 145:146], self.P6[:, 21:22]), after=[e_b2], sig="VC")
        p.step("vector", lambda v: v.tensor_scalar(
            C2[:, 2096:2104], CF[:, 136:144], CF[:, 144:145], CF[:, 145:146], ALU.subtract, ALU.mult),
            after=[e_muc, e_rsc], sig="VC")
        self.e_utmask = e_cd
        self.p.barrier()

    def ln_tok(self, x_in):
        p, CF = self.p, self.CF
        xt_cols = [PH, PH + 2048]
        xn_cols = [PH + 4096, PH + 5120]
        sq_cols = PH + 6144
        last_xt_reader = [None, None]
        prev_store = [None, None]
        prev_sq = None
        last_stats = [None, None]
        e_store = []
        for j in range(NCH):
            par = j % 2
            xt = self.f32(xt_cols[par], 1024)
            xn = self.bf(xn_cols[par], 1024)
            sb = 88 + 8 * par
            e_ld = p.step("sync", lambda s, j=j, xt=xt: s.dma_start(out=xt, in_=x_in[128 * j:128 * (j + 1), :]),
                          after=[last_xt_reader[par]], sig="SQ", dma=True)
            e_sum = p.step("vector", lambda v, xt=xt, sb=sb: v.reduce_sum(CF[:, sb:sb + 1], xt, axis=AX.X),
                           after=[e_ld, last_stats[par]], sig="VC")
            e_sq = p.step("scalar", lambda s, xt=xt: s.activation(self.f32(sq_cols, 1024), xt, AF.Square),
                          after=[e_ld, prev_sq], sig="SC")
            e_sum2 = p.step("vector", lambda v, sb=sb: v.reduce_sum(CF[:, sb + 1:sb + 2], self.f32(sq_cols, 1024), axis=AX.X),
                            after=[e_sq], sig="VC")
            prev_sq = e_sum2
            e_mu = p.step("vector", lambda v, sb=sb: v.tensor_scalar(CF[:, sb + 2:sb + 3], CF[:, sb:sb + 1], -1.0 / 1024.0, None, ALU.mult),
                          after=[e_sum], sig="VC")
            e_mu2 = p.step("vector", lambda v, sb=sb: v.tensor_tensor(CF[:, sb + 3:sb + 4], CF[:, sb + 2:sb + 3], CF[:, sb + 2:sb + 3], ALU.mult),
                           after=[e_mu], sig="VC")
            e_var = p.step("vector", lambda v, sb=sb: v.scalar_tensor_tensor(
                CF[:, sb + 4:sb + 5], CF[:, sb + 1:sb + 2], 1.0 / 1024.0, CF[:, sb + 3:sb + 4], ALU.mult, ALU.subtract),
                after=[e_mu2, e_sum2], sig="VC")
            e_lnr = p.step("scalar", lambda s, sb=sb: s.activation(CF[:, sb + 5:sb + 6], CF[:, sb + 4:sb + 5], AF.Ln, bias=CF[:, 146:147]),
                           after=[e_var], sig="SC")
            e_rstd = p.step("scalar", lambda s, sb=sb: s.activation(CF[:, sb + 6:sb + 7], CF[:, sb + 5:sb + 6], AF.Exp, scale=-0.5),
                            after=[e_lnr], sig="SC")
            e_xn = p.step("vector", lambda v, xt=xt, xn=xn, sb=sb: v.tensor_scalar(
                xn, xt, CF[:, sb + 2:sb + 3], CF[:, sb + 6:sb + 7], ALU.add, ALU.mult),
                after=[e_rstd, prev_store[par]], sig="VC")
            last_xt_reader[par] = e_xn
            last_stats[par] = e_xn
            ev = p.step("scalar", lambda s, j=j, xn=xn: s.dma_start(out=self.scrA[128 * j:128 * (j + 1), :], in_=xn),
                        after=[e_xn], sig="AQ", dma=True)
            prev_store[par] = ev
            e_store.append(ev)
        e_all = e_store[-1]
        e_tl = []
        for c in range(8):
            e_tl.append(p.step("sync", lambda s, c=c: s.dma_start(
                out=self.bf(XN + 2048 * c, 2048), in_=self.scrA[:, 128 * c:128 * (c + 1)], transpose=True),
                after=[e_all], sig="SQ", dma=True))
        return e_tl

    def v_proj(self, xn_tl, first):
        p = self.p
        e_wv = p.step("gpsimd", lambda g: g.dma_start(
            out=self.bf(QD, 8192).rearrange("p (kc c) -> p kc c", kc=8),
            in_=self.wv.rearrange("(kc p) c -> p kc c", p=128)), sig="LD", dma=True)
        e_ones = None
        if first:
            for j in range(NCH):
                ap = self.A[:, VT + 1056 * j + 64: VT + 1056 * j + 64 + 16 * 66:66]
                e_ones = p.step("gpsimd", lambda g, ap=ap: g.memset(ap, 1.0), sig="PC")
        evs = []
        prev_evac = [None, None]
        for j in range(NCH):
            par = j % 2
            pb = self.PBG[:, 1024 * par:1024 * (par + 1)]
            e_mm = None
            for dh in range(2):
                for kc in range(8):
                    e_mm = p.step("tensor", lambda t, j=j, dh=dh, kc=kc, pb=pb: t.matmul(
                        pb[:, 512 * dh:512 * (dh + 1)],
                        self.bf(XN + 2048 * kc + 128 * j, 128),
                        self.bf(QD + 1024 * kc + 512 * dh, 512),
                        start=(kc == 0), stop=(kc == 7)),
                        after=[e_wv, xn_tl[kc], prev_evac[par]],
                        sig="TC" if (dh == 1 and kc == 7) else None)
            dst = self.A[:, VT + 1056 * j: VT + 1056 * j + 16 * 66].rearrange("p (h c) -> p h c", h=16)[:, :, 0:64]
            src = pb.rearrange("p (h c) -> p h c", h=16)
            ev = p.step("vector", lambda v, dst=dst, src=src: v.tensor_copy(dst, src),
                        after=[e_mm, e_ones], sig="VC")
            prev_evac[par] = ev
            evs.append(ev)
        return evs

    def load_w_slice(self, w_packed, idx, ncols, slot_cols, slot_id, after=()):
        src = w_packed[128 * idx:128 * (idx + 1), :]
        dst = self.bf(slot_cols, 8 * ncols)
        ev = self.p.step("gpsimd", lambda g: g.dma_start(out=dst, in_=src),
                         after=list(after) + [self.ws_user[slot_id]], sig="LD", dma=True)
        return ev

    def proj_pair(self, w_ev, wslot, slot_id, xn_tl, scale):
        p = self.p
        e_list = []
        last_mm = None
        for ci in range(2):
            for tt in range(4):
                bi = (4 * ci + tt) % 4
                bank = self.PBG[:, 512 * bi: 512 * (bi + 1)]
                idx = 4 * ci + tt
                e_mm = None
                for kc in range(8):
                    e_mm = p.step("tensor", lambda t, ci=ci, tt=tt, kc=kc, bank=bank, wslot=wslot: t.matmul(
                        bank, self.bf(wslot + 256 * kc + 128 * ci, 128),
                        self.bf(XN + 2048 * kc + 512 * tt, 512),
                        start=(kc == 0), stop=(kc == 7)),
                        after=[w_ev, xn_tl[kc]] + ([e_list[idx - 4][0]] if idx >= 4 else [])
                        + ([self.phi_pbig[0], self.phi_pbig[1]] if idx == 0 else []),
                        sig="TC" if kc == 7 else None)
                e_e = p.step("scalar", lambda s, ci=ci, tt=tt, bank=bank: s.activation(
                    self.bf(QD + 8192 + 2048 * ci + 512 * tt, 512), bank, AF.Copy, scale=scale),
                    after=[e_mm], sig="SC")
                e_q = p.step("vector", lambda v, ci=ci, tt=tt: v.tensor_tensor(
                    self.bf(QD + 12288 + 2048 * ci + 512 * tt, 512),
                    self.bf(QD + 8192 + 2048 * ci + 512 * tt, 512),
                    self.bf(QD + 8192 + 2048 * ci + 512 * tt, 512), ALU.mult),
                    after=[e_e], sig="VC")
                e_list.append((e_e, e_q))
                last_mm = e_mm
        self.ws_user[slot_id] = last_mm
        return e_list[-1][0], e_list[-1][1]

    def phi(self, h, out_tok_cols, outM_cols, rot, want_M, after_proj):
        p, C, CF = self.p, self.C, self.CF
        odd = h % 2
        ci = (h // 2) % 2
        r0 = 64 * odd
        par2 = rot % 2
        pM = PH + 16384 if par2 == 0 else XS + 2048
        ptk = PH + 18432 if par2 == 0 else XS + 4096
        qtk = pM
        mx0, sr0 = (104, 120) if par2 == 0 else (440, 456)
        otok = out_tok_cols if out_tok_cols is not None else qtk
        rr = rot % self.NROT

        e_exp = []
        for half in range(2):
            pb = self.PBG[:, 1024 * half:1024 * (half + 1)]
            e_mm = None
            for blk in range(2):
                cols = 1024 * half + 512 * blk
                p.step("tensor", lambda t, pb=pb, blk=blk, cols=cols: t.matmul(
                    pb[:, 512 * blk:512 * (blk + 1)],
                    C[r0:r0 + 64, 260:388],
                    self.A[r0:r0 + 64, QD + 8192 + 2048 * ci + cols: QD + 8192 + 2048 * ci + cols + 512],
                    start=True, stop=False),
                    after=list(after_proj) + [self.phi_pbig[half]], sig=None)
                e_mm = p.step("tensor", lambda t, pb=pb, blk=blk, cols=cols: t.matmul(
                    pb[:, 512 * blk:512 * (blk + 1)],
                    C[r0:r0 + 64, 388:516],
                    self.A[r0:r0 + 64, QD + 12288 + 2048 * ci + cols: QD + 12288 + 2048 * ci + cols + 512],
                    start=False, stop=True), sig="TC")
            e_exp.append(p.step("scalar", lambda s, half=half, pb=pb, pM=pM: s.activation(
                self.bf(pM + 1024 * half, 1024), pb, AF.Exp),
                after=[e_mm, self.pM_user[par2]], sig="SC"))
            self.phi_pbig[half] = e_exp[half]
        if h == 1 and want_M and out_tok_cols is not None:
            self.tap("pm1", self.bf(pM, 2048), e_exp)
        pd = self.pscr[rr]
        o3 = pd.rearrange("(c m) p -> c m p", m=128).transpose([1, 0, 2])
        i3 = self.bf(pM, 2048).rearrange("m (c q) -> m c q", q=128)
        e_st = p.step("scalar", lambda s, o3=o3, i3=i3: s.dma_start(out=o3, in_=i3),
                      after=e_exp + [self.pscr_free[rr]], sig="AQ", dma=True)
        e_tlp = p.step("sync", lambda s, pd=pd, ptk=ptk: s.dma_start(out=self.bf(ptk, 2048), in_=pd, transpose=True),
                       after=[e_st, self.ptk_user[par2]], sig="SQ", dma=True)
        self.pscr_free[rr] = None
        p3 = self.bf(ptk, 2048).rearrange("p (c m) -> p c m", m=128)
        e_mx = p.step("vector", lambda v, p3=p3, mx0=mx0: v.tensor_reduce(CF[:, mx0:mx0 + 16], p3, AX.X, ALU.max),
                      after=[e_tlp, self.stats_user[par2]], sig="VC")
        e_s1 = p.step("vector", lambda v, mx0=mx0, sr0=sr0: v.tensor_scalar(CF[:, sr0:sr0 + 16], CF[:, mx0:mx0 + 16], SQM, None, ALU.mult),
                      after=[e_mx], sig="VC")
        e_s2 = p.step("vector", lambda v, sr0=sr0: v.reciprocal(CF[:, sr0:sr0 + 16], CF[:, sr0:sr0 + 16]), after=[e_s1], sig="VC")
        sb3 = CF[:, sr0:sr0 + 16][:, :, None].broadcast_to([128, 16, 128])
        ot3 = self.bf(otok, 2048).rearrange("p (c m) -> p c m", m=128)
        e_t1 = p.step("vector", lambda v, ot3=ot3, p3=p3, sb3=sb3: v.tensor_tensor(ot3, p3, sb3, ALU.mult),
                      after=[e_s2] + ([self.qtk_user] if out_tok_cols is None else []), sig="VC")
        self.pscr_free[rr] = e_t1
        self.ptk_user[par2] = e_t1
        self.stats_user[par2] = e_t1
        e_tok = p.step("vector", lambda v, otok=otok: v.tensor_scalar(
            self.bf(otok, 2048), self.bf(otok, 2048), FEPS, None, ALU.add),
            after=[e_t1], sig="VC")
        if h == 1 and want_M and out_tok_cols is not None:
            self.tap("ptok1", self.bf(ptk, 2048), [e_t1])
            self.tap("ktok1", self.bf(otok, 2048), [e_tok])
        e_M = None
        if want_M:
            qd = self.qscr[rr]
            o3b = qd.rearrange("(c q) m -> c q m", q=128).transpose([1, 0, 2])
            i3b = self.bf(otok, 2048).rearrange("q (c m) -> q c m", m=128)
            if rot == TAPROT:
                tq1 = self.nc.dram_tensor("dbg_qtok", [128, 2048], BF16, kind="ExternalOutput").ap()
                p.step("sync", lambda s, tq1=tq1, otok=otok: s.dma_start(out=tq1, in_=self.bf(otok, 2048)),
                       after=[e_tok], sig="SQ", dma=True)
            e_st2 = p.step("gpsimd", lambda g, o3b=o3b, i3b=i3b: g.dma_start(out=o3b, in_=i3b),
                           after=[e_tok, self.qscr_free[rr]], sig="LD", dma=True)
            if rot == TAPROT:
                tq2 = self.nc.dram_tensor("dbg_qscr", [T, 128], BF16, kind="ExternalOutput").ap()
                p.step("sync", lambda s, tq2=tq2, qd=qd: s.dma_start(out=tq2, in_=qd),
                       after=[e_st2], sig="SQ", dma=True)
                tq3 = self.nc.dram_tensor("dbg_ptok2", [128, 2048], BF16, kind="ExternalOutput").ap()
                p.step("sync", lambda s, tq3=tq3: s.dma_start(out=tq3, in_=self.bf(PH + 18432, 2048)),
                       after=[e_tok], sig="SQ", dma=True)
            e_M = p.step("sync", lambda s, qd=qd: s.dma_start(out=self.bf(outM_cols, 2048), in_=qd, transpose=True),
                         after=[e_st2], sig="SQ", dma=True)
            self.qscr_free[rr] = e_M
            if out_tok_cols is None:
                self.qtk_user = e_st2
                self.pM_user[par2] = e_st2
        if out_tok_cols is not None or e_M is None:
            self.pM_user[par2] = e_st
        return e_tok, e_M

    def build(self):
        p = self.p
        self.consts()
        rot = 0

        xp_tl = self.ln_tok(self.xp_tok)
        vp_evs = self.v_proj(xp_tl, first=True)
        p.barrier()
        e_sinit_last = None
        for g in range(4):
            e_wk = self.load_w_slice(self.wkp, g, 256, WS + 2048 * (g % 2), g % 2)
            pe_e, pe_q = self.proj_pair(e_wk, WS + 2048 * (g % 2), g % 2, xp_tl, SCALE)
            e_tok4 = []
            for hl in range(4):
                e_tok, _ = self.phi(4 * g + hl, PH + 8192 + 2048 * hl, None, rot, False, [pe_e, pe_q])
                e_tok4.append(e_tok)
                rot += 1
            for hl in range(4):
                e_mm = None
                for j in range(NCH):
                    e_mm = p.step("tensor", lambda t, hl=hl, j=j, g=g: t.matmul(
                        self.P4[:, 65 * hl:65 * hl + 65],
                        self.bf(PH + 8192 + 2048 * hl + 128 * j, 128),
                        self.A[:, VT + 1056 * j + 66 * (4 * g + hl): VT + 1056 * j + 66 * (4 * g + hl) + 65],
                        start=(j == 0), stop=(j == NCH - 1)),
                        after=([e_tok4[hl], vp_evs[j], e_sinit_last] if j == 0 else [vp_evs[j]]),
                        sig="TC" if j == NCH - 1 else None)
                dst = self.C2[:, 66 * (4 * g + hl): 66 * (4 * g + hl) + 65]
                e_sinit_last = p.step("vector", lambda v, dst=dst, hl=hl: v.tensor_copy(dst, self.P4[:, 65 * hl:65 * hl + 65]),
                                      after=[e_mm], sig="VC")
        e_sim = p.step("vector", lambda v: v.tensor_scalar(
            self.C2[:, 0:1056], self.C2[:, 0:1056], self.CF[:, 80:81], None, ALU.mult),
            after=[e_sinit_last], sig="VC")
        self.tap("sinit", self.C2[:, 0:1056], [e_sim])
        p.barrier()

        x_tl = self.ln_tok(self.x_tok)
        self.tap("xn", self.bf(XN, 16384), [x_tl[7]])
        v_evs = self.v_proj(x_tl, first=False)
        self.tap("vtok", self.bf(VT, 16896), [v_evs[-1]])
        p.barrier()

        e_attn_last = None
        for g in range(4):
            e_wk = self.load_w_slice(self.wkp, g, 256, WS + 2048 * (g % 2), g % 2)
            pe_e, pe_q = self.proj_pair(e_wk, WS + 2048 * (g % 2), g % 2, x_tl, SCALE)
            if g == 0:
                self.tap("kfm", self.bf(QD + 8192, 4096), [pe_e, pe_q])
            e_ktok, e_kM = [], []
            for hl in range(4):
                e_tok, e_M = self.phi(4 * g + hl, PH + 8192 + 2048 * hl, PH + 2048 * hl, rot, True, [pe_e, pe_q])
                if g == 0 and hl == 0:
                    self.tap("kptok", self.bf(PH + 8192, 2048), [e_tok])
                    self.tap("kpM", self.bf(PH, 2048), [e_M])
                e_ktok.append(e_tok)
                e_kM.append(e_M)
                rot += 1
            e_sr = p.step("vector", lambda v, g=g: v.tensor_copy(
                self.CF[:, 160:424], self.C2[:, 264 * g:264 * g + 264]),
                after=e_ktok, sig="VC")
            e_sn = []
            e_add = e_sr
            for j in range(NCH):
                sn0 = SN + 4224 * g
                dst = self.A[:, sn0: sn0 + 4224].rearrange("p (hh c) -> p hh c", hh=4)[:, :, 66 * j:66 * j + 65]
                srun3 = self.CF[:, 160:424].rearrange("p (hh c) -> p hh c", hh=4)[:, :, 0:65]
                e_snap = p.step("vector", lambda v, dst=dst, srun3=srun3: v.tensor_copy(dst, srun3),
                                after=[e_add], sig="VC")
                e_mm = None
                for hl in range(4):
                    e_mm = p.step("tensor", lambda t, hl=hl, j=j, g=g: t.matmul(
                        self.P4[:, 65 * hl:65 * hl + 65],
                        self.bf(PH + 8192 + 2048 * hl + 128 * j, 128),
                        self.A[:, VT + 1056 * j + 66 * (4 * g + hl): VT + 1056 * j + 66 * (4 * g + hl) + 65],
                        start=True, stop=True),
                        after=([e_ktok[hl]] if j == 0 else []) + ([e_add] if hl == 0 else []),
                        sig="TC" if hl == 3 else None)
                ps3 = self.P4[:, 0:260].rearrange("p (hh c) -> p hh c", hh=4)[:, :, 0:65]
                e_add = p.step("vector", lambda v, srun3=srun3, ps3=ps3: v.tensor_tensor(srun3, srun3, ps3, ALU.add),
                               after=[e_mm, e_snap], sig="VC")
                e_sn.append(e_snap)
            e_state_done = e_add
            if g == 0:
                self.tap("snap", self.bf(SN, 4224), [e_state_done])
            if g == 1:
                self.tap("snap1", self.bf(SN + 4224, 4224), [e_state_done])
                self.tap("ktokq1", self.bf(PH + 8192, 8192), e_ktok)
                self.tap("kMq1", self.bf(PH, 8192), e_kM)
            qslot = (g + 1) % 2
            e_wq = self.load_w_slice(self.wqp, g, 256, WS + 2048 * qslot, qslot)
            pq_e, pq_q = self.proj_pair(e_wq, WS + 2048 * qslot, qslot, x_tl, SCALE)
            if g == 1 and "qfm2" in TAPS:
                tq4 = self.nc.dram_tensor("dbg_qfm2", [128, 4096], BF16, kind="ExternalOutput").ap()
                p.step("sync", lambda s, tq4=tq4: s.dma_start(out=tq4, in_=self.bf(QD + 8192, 4096)),
                       after=[pq_e, pq_q], sig="SQ", dma=True)
                tq5 = self.nc.dram_tensor("dbg_qsq2", [128, 4096], BF16, kind="ExternalOutput").ap()
                p.step("sync", lambda s, tq5=tq5: s.dma_start(out=tq5, in_=self.bf(QD + 12288, 4096)),
                       after=[pq_e, pq_q], sig="SQ", dma=True)
                tq6 = self.nc.dram_tensor("dbg_slot0", [128, 2048], BF16, kind="ExternalOutput").ap()
                p.step("sync", lambda s, tq6=tq6, qslot=qslot: s.dma_start(out=tq6, in_=self.bf(WS + 2048 * qslot, 2048)),
                       after=[pq_e, pq_q], sig="SQ", dma=True)
            e_qM = []
            for hl in range(4):
                _, e_M = self.phi(4 * g + hl, None, PH + 8192 + 2048 * hl, rot, True,
                                  [pq_e, pq_q, e_state_done])
                if g == 0 and hl == 0:
                    self.tap("qpM", self.bf(PH + 8192, 2048), [e_M])
                if g == 1 and hl == 3:
                    self.tap("qMq1", self.bf(PH + 8192, 8192), [e_M])
                e_qM.append(e_M)
                rot += 1
            amt = [XS, XS + 512]
            atn = [XS + 1024, XS + 1280]
            prev_eam = None
            prev_amt_user = [None, None]
            prev_atn_user = [None, None]
            prev_numuser = None
            prev_den = None
            for j in range(NCH):
                par = j % 2
                e_at = None
                for hl in range(4):
                    e_at = p.step("tensor", lambda t, hl=hl, j=j: t.matmul(
                        self.P5[:, 128 * hl:128 * (hl + 1)],
                        self.bf(PH + 2048 * hl + 128 * j, 128),
                        self.bf(PH + 8192 + 2048 * hl + 128 * j, 128),
                        start=True, stop=True),
                        after=([e_kM[hl], e_qM[hl]] if j < 2 else []) + ([prev_eam] if hl == 0 else []),
                        sig="TC" if hl == 3 else None)
                ps3 = self.P5[:, 0:512].rearrange("p (hh c) -> p hh c", hh=4)
                mb = self.C[:, 0:128][:, None, :].broadcast_to([128, 4, 128])
                am3 = self.bf(amt[par], 512).rearrange("p (hh c) -> p hh c", hh=4)
                e_am = p.step("vector", lambda v, am3=am3, ps3=ps3, mb=mb: v.tensor_tensor(am3, ps3, mb, ALU.mult),
                              after=[e_at, self.e_utmask, prev_amt_user[par]], sig="VC")
                prev_eam = e_am
                e_nm = None
                for hl in range(4):
                    p.step("tensor", lambda t, hl=hl, j=j, g=g, par=par: t.matmul(
                        self.P6[:, 65 * hl:65 * hl + 65],
                        self.bf(amt[par] + 128 * hl, 128),
                        self.A[:, VT + 1056 * j + 66 * (4 * g + hl): VT + 1056 * j + 66 * (4 * g + hl) + 65],
                        start=True, stop=False),
                        after=[e_am] + ([prev_numuser] if hl == 0 else []), sig=None)
                    e_nm = p.step("tensor", lambda t, hl=hl, j=j, g=g: t.matmul(
                        self.P6[:, 65 * hl:65 * hl + 65],
                        self.bf(PH + 8192 + 2048 * hl + 128 * j, 128),
                        self.A[:, SN + 1056 * (4 * g + hl) + 66 * j: SN + 1056 * (4 * g + hl) + 66 * j + 65],
                        start=False, stop=True),
                        after=[e_sn[j]] if hl == 0 else [], sig="TC" if hl == 3 else None)
                prev_amt_user[par] = e_nm
                den3 = self.P6[:, 0:260].rearrange("p (hh c) -> p hh c", hh=4)[:, :, 64:65]
                if g == 1 and j == 0 and "ps6" in TAPS:
                    t14 = self.nc.dram_tensor("dbg_amtl", [128, 512], BF16, kind="ExternalOutput").ap()
                    p.step("sync", lambda s, t14=t14, par=par: s.dma_start(out=t14, in_=self.bf(amt[par], 512)),
                           after=[e_nm], sig="SQ", dma=True)
                    t15 = self.nc.dram_tensor("dbg_snapl", [128, 260], BF16, kind="ExternalOutput").ap()
                    p.step("sync", lambda s, t15=t15, g=g, j=j: s.dma_start(
                        out=t15, in_=self.A[:, SN + 4224 * g + 66 * j: SN + 4224 * g + 66 * j + 260]),
                        after=[e_nm], sig="SQ", dma=True)
                    t16b = self.nc.dram_tensor("dbg_qpml", [128, 128], BF16, kind="ExternalOutput").ap()
                    p.step("sync", lambda s, t16b=t16b, j=j: s.dma_start(out=t16b, in_=self.bf(PH + 8192 + 128 * j, 128)),
                           after=[e_nm], sig="SQ", dma=True)
                    t12 = self.nc.dram_tensor("dbg_ps6", [128, 260], F32, kind="ExternalOutput").ap()
                    e_cp6 = p.step("vector", lambda v: v.tensor_copy(self.f32(XS + 4096, 260), self.P6[:, 0:260]),
                                   after=[e_nm], sig="VC")
                    p.step("sync", lambda s, t12=t12: s.dma_start(out=t12, in_=self.f32(XS + 4096, 260)),
                           after=[e_cp6], sig="SQ", dma=True)
                e_d1 = p.step("vector", lambda v, den3=den3: v.tensor_scalar(self.CF[:, 424:428], den3, FEPS, None, ALU.add),
                              after=[e_nm, prev_den], sig="VC")
                e_d2 = p.step("vector", lambda v: v.reciprocal(self.CF[:, 428:432], self.CF[:, 424:428]), after=[e_d1], sig="VC")
                num3 = self.P6[:, 0:260].rearrange("p (hh c) -> p hh c", hh=4)[:, :, 0:64]
                rb3 = self.CF[:, 428:432][:, :, None].broadcast_to([128, 4, 64])
                at3 = self.bf(atn[par], 256).rearrange("p (hh c) -> p hh c", hh=4)
                e_at2 = p.step("vector", lambda v, at3=at3, num3=num3, rb3=rb3: v.tensor_tensor(at3, num3, rb3, ALU.mult),
                               after=[e_d2, prev_atn_user[par]], sig="VC")
                if g == 1 and j == 0 and "ps6" in TAPS:
                    t13 = self.nc.dram_tensor("dbg_rden", [128, 16], F32, kind="ExternalOutput").ap()
                    p.step("sync", lambda s, t13=t13: s.dma_start(out=t13, in_=self.CF[:, 424:440]),
                           after=[e_at2], sig="SQ", dma=True)
                prev_numuser = e_at2
                prev_den = e_at2
                e_flush = p.step("vector", lambda v: v.memset(self.CF[:, 439:440], 0.0),
                                 after=[e_at2], sig="VC")
                ev = p.step("scalar", lambda s, j=j, g=g, par=par: s.dma_start(
                    out=self.scrA[128 * j:128 * (j + 1), 256 * g:256 * (g + 1)], in_=self.bf(atn[par], 256)),
                    after=[e_flush], sig="AQ", dma=True)
                if g == 1 and j == 0 and "sc10" in TAPS:
                    t10 = self.nc.dram_tensor("dbg_sc10", [128, 256], BF16, kind="ExternalOutput").ap()
                    t11 = self.nc.dram_tensor("dbg_sb10", [128, 256], BF16, kind="ExternalOutput").ap()
                    p.step("sync", lambda s, t10=t10: s.dma_start(out=t10, in_=self.scrA[0:128, 256:512]),
                           after=[ev], sig="SQ", dma=True)
                    p.step("sync", lambda s, t11=t11, par=par: s.dma_start(out=t11, in_=self.bf(atn[par], 256)),
                           after=[ev], sig="SQ", dma=True)
                prev_atn_user[par] = ev
                e_attn_last = ev
            p.barrier()

        e_afm = []
        for c in range(8):
            e_afm.append(p.step("sync", lambda s, c=c: s.dma_start(
                out=self.bf(QD + 2048 * c, 2048), in_=self.scrA[:, 128 * c:128 * (c + 1)], transpose=True),
                after=[e_attn_last], sig="SQ", dma=True))
        if "scra" in TAPS:
            t = self.nc.dram_tensor("dbg_scra", [T, D], BF16, kind="ExternalOutput").ap()
            self.p.step("sync", lambda s, t=t: s.dma_start(out=t, in_=self.scrA), after=[e_attn_last], sig="SQ", dma=True)
            self.p.barrier()
        self.tap("attnfm", self.bf(QD, 16384), [e_afm[7]])
        e_y = {}
        for c in range(8):
            slot = c % 2
            e_w = self.load_w_slice(self.wop, c, 128, WS + 2048 * slot, slot)
            last_mm = None
            for tt in range(4):
                bank = self.PBG[:, 512 * tt:512 * (tt + 1)]
                e_mm = None
                for kc in range(8):
                    e_mm = p.step("tensor", lambda t, c=c, tt=tt, kc=kc, bank=bank, slot=slot: t.matmul(
                        bank, self.bf(WS + 2048 * slot + 128 * kc, 128),
                        self.bf(QD + 2048 * kc + 512 * tt, 512),
                        start=(kc == 0), stop=(kc == 7)),
                        after=[e_w, e_afm[kc]] + ([e_y.get((c - 1, tt))] if c >= 1 else []),
                        sig="TC" if kc == 7 else None)
                e_y[(c, tt)] = p.step("vector", lambda v, c=c, tt=tt, bank=bank: v.tensor_tensor(
                    self.bf(PH + 2048 * c + 512 * tt, 512), bank,
                    self.bf(XN + 2048 * c + 512 * tt, 512), ALU.add),
                    after=[e_mm], sig="VC")
                last_mm = e_mm
            self.ws_user[slot] = last_mm
        self.tap("y", self.bf(PH, 16384), [e_y[(7, 3)]])
        p.barrier()

        muR = self.f32(WS, 2048)[0:1, :]
        s2R = self.f32(XS + 6144, 2048)[0:1, :]
        e_rowmu = []
        e_rows2 = []
        prev_sqv = None
        for tt in range(4):
            e_m1 = None
            e_m2 = None
            for c in range(8):
                e_q = p.step("scalar", lambda s, c=c, tt=tt: s.activation(
                    self.bf(XS + 4096, 512), self.bf(PH + 2048 * c + 512 * tt, 512), AF.Square),
                    after=[e_y[(c, tt)], prev_sqv], sig="SC")
                e_m1 = p.step("tensor", lambda t, c=c, tt=tt: t.matmul(
                    self.P7[0:1, 0:512], self.C[:, 128:129], self.bf(PH + 2048 * c + 512 * tt, 512),
                    start=(c == 0), stop=(c == 7)),
                    after=[e_y[(c, tt)]] + ([e_rowmu[-1]] if (c == 0 and e_rowmu) else []),
                    sig="TC" if c == 7 else None)
                e_m2 = p.step("tensor", lambda t, c=c, tt=tt: t.matmul(
                    self.P6[0:1, 0:512], self.C[:, 128:129], self.bf(XS + 4096, 512),
                    start=(c == 0), stop=(c == 7)),
                    after=[e_q] + ([e_rows2[-1]] if (c == 0 and e_rows2) else []),
                    sig="TC" if c == 7 else None)
                prev_sqv = e_m2
            e_rowmu.append(p.step("vector", lambda v, tt=tt: v.tensor_copy(muR[0:1, 512 * tt:512 * (tt + 1)], self.P7[0:1, 0:512]),
                                  after=[e_m1], sig="VC"))
            e_rows2.append(p.step("vector", lambda v, tt=tt: v.tensor_copy(s2R[0:1, 512 * tt:512 * (tt + 1)], self.P6[0:1, 0:512]),
                                  after=[e_m2], sig="VC"))
        vR = s2R
        e_nm = p.step("vector", lambda v: v.tensor_scalar(muR, muR, -1.0 / 1024.0, None, ALU.mult),
                      after=[e_rowmu[-1]], sig="VC")
        mu2R = self.f32(XS, 2048)[0:1, :]
        e_mu2 = p.step("vector", lambda v: v.tensor_tensor(mu2R, muR, muR, ALU.mult), after=[e_nm], sig="VC")
        e_v = p.step("vector", lambda v: v.scalar_tensor_tensor(vR, s2R, 1.0 / 1024.0, mu2R, ALU.mult, ALU.subtract),
                     after=[e_mu2, e_rows2[-1]], sig="VC")
        e_l = p.step("scalar", lambda s: s.activation(vR, vR, AF.Ln, bias=self.CF[0:1, 147:148]), after=[e_v], sig="SC")
        e_r = p.step("scalar", lambda s: s.activation(self.bf(PH + 16384, 2048)[0:1, :], vR, AF.Exp, scale=-0.5),
                     after=[e_l], sig="SC")
        e_br = p.step("vector", lambda v: v.tensor_tensor(
            self.bf(PH + 18432, 2048)[0:1, :], muR, self.bf(PH + 16384, 2048)[0:1, :], ALU.mult),
            after=[e_r, e_nm], sig="VC")
        e_bc = []
        for (src0, dst0) in ((PH + 16384, XS), (PH + 18432, XS + 2048)):
            for tt in range(4):
                e_mm = p.step("tensor", lambda t, src0=src0, tt=tt: t.matmul(
                    self.P7[:, 0:512], self.C[0:1, 129:257], self.bf(src0, 2048)[0:1, 512 * tt:512 * (tt + 1)],
                    start=True, stop=True),
                    after=[e_r if src0 == PH + 16384 else e_br, self.e_ones] + ([e_bc[-1]] if e_bc else []),
                    sig="TC")
                e_bc.append(p.step("vector", lambda v, dst0=dst0, tt=tt: v.tensor_copy(
                    self.bf(dst0 + 512 * tt, 512), self.P7[:, 0:512]), after=[e_mm], sig="VC"))
        e_bcl = e_bc[-1]
        e_h2 = []
        for c in range(8):
            e_1 = p.step("vector", lambda v, c=c: v.tensor_tensor(
                self.bf(H2 + 2048 * c, 2048), self.bf(PH + 2048 * c, 2048), self.bf(XS, 2048), ALU.mult),
                after=[e_bcl, e_y[(c, 3)]] + ([e_h2[-1]] if e_h2 else []), sig="VC")
            e_2 = p.step("vector", lambda v, c=c: v.tensor_tensor(
                self.bf(H2 + 2048 * c, 2048), self.bf(H2 + 2048 * c, 2048), self.bf(XS + 2048, 2048), ALU.add),
                after=[e_1], sig="VC")
            e_h2.append(p.step("scalar", lambda s, c=c: s.activation(
                self.bf(H2 + 2048 * c, 2048), self.bf(H2 + 2048 * c, 2048), AF.Identity,
                bias=self.CF[:, 8 + c:9 + c], scale=self.CF[:, 0 + c:1 + c]),
                after=[e_2], sig="SC"))
        self.tap("h2", self.bf(H2, 16384), [e_h2[-1]])
        p.barrier()

        tmp = XS
        prev_c = None
        e_mix = []
        for (mu0, om0, dst0) in ((16, 32, PH), (24, 40, QD)):
            for c in range(8):
                e_a = p.step("vector", lambda v, c=c, om0=om0: v.tensor_scalar(
                    self.bf(tmp + 1, 2047), self.bf(XN + 2048 * c, 2047), self.CF[:, om0 + c:om0 + c + 1], None, ALU.mult),
                    after=[prev_c], sig="VC")
                e_b = p.step("vector", lambda v, c=c, om0=om0: v.tensor_scalar(
                    self.bf(tmp, 1), self.C2[:, 2096 + c:2097 + c], self.CF[:, om0 + c:om0 + c + 1], None, ALU.mult),
                    after=[e_a], sig="VC")
                e_c = p.step("vector", lambda v, c=c, mu0=mu0, dst0=dst0: v.scalar_tensor_tensor(
                    self.bf(dst0 + 2048 * c, 2048), self.bf(XN + 2048 * c, 2048),
                    self.CF[:, mu0 + c:mu0 + c + 1], self.bf(tmp, 2048), ALU.mult, ALU.add),
                    after=[e_b], sig="VC")
                prev_c = e_c
                e_mix.append(e_c)
        self.tap("xr", self.bf(PH, 16384), [e_mix[-1]])
        self.tap("xk", self.bf(QD, 16384), [e_mix[-1]])
        p.barrier()
        for (wsrc, xsrc, kind) in ((self.wk2p, QD, "k"), (self.wrtp, PH, "r")):
            for c in range(8):
                slot = c % 2
                e_w = self.load_w_slice(wsrc, c, 128, WS + 2048 * slot, slot)
                last_mm = None
                for tt in range(4):
                    bank = self.PBG[:, 512 * tt:512 * (tt + 1)]
                    e_mm = None
                    for kc in range(8):
                        e_mm = p.step("tensor", lambda t, c=c, tt=tt, kc=kc, bank=bank, slot=slot, xsrc=xsrc: t.matmul(
                            bank, self.bf(WS + 2048 * slot + 128 * kc, 128),
                            self.bf(xsrc + 2048 * kc + 512 * tt, 512),
                            start=(kc == 0), stop=(kc == 7)),
                            after=[e_w, e_mix[-1]] + ([self.kk_bank_user.get((c - 1, tt))] if c >= 1 else []),
                            sig="TC" if kc == 7 else None) if True else None
                    if kind == "k":
                        e_rl = p.step("vector", lambda v, c=c, tt=tt, bank=bank: v.tensor_scalar(
                            self.bf(XS + 4096, 512), bank, 0.0, None, ALU.max),
                            after=[e_mm, self.kk_tmp_user], sig="VC")
                        e_kq = p.step("vector", lambda v, c=c, tt=tt: v.tensor_tensor(
                            self.bf(XN + 2048 * c + 512 * tt, 512), self.bf(XS + 4096, 512), self.bf(XS + 4096, 512), ALU.mult),
                            after=[e_rl], sig="VC")
                        self.kk_tmp_user = e_kq
                        self.kk_bank_user[(c, tt)] = e_kq
                    else:
                        e_sg = p.step("scalar", lambda s, c=c, tt=tt, bank=bank: s.activation(
                            self.bf(SN + 2048 * c + 512 * tt, 512), bank, AF.Sigmoid),
                            after=[e_mm], sig="SC")
                        self.kk_bank_user[(c, tt)] = e_sg
                        ev = p.step("scalar", lambda s, c=c, tt=tt: s.dma_start(
                            out=self.rscr[128 * c:128 * (c + 1), 512 * tt:512 * (tt + 1)],
                            in_=self.bf(SN + 2048 * c + 512 * tt, 512)),
                            after=[e_sg], sig="AQ", dma=True)
                        self.r_store = ev
                    last_mm = e_mm
                self.ws_user[slot] = last_mm
            p.barrier()
        self.tap("kact", self.bf(XN, 16384), [self.kk_tmp_user])
        e_rt = []
        for tc in range(NCH):
            e_rt.append(p.step("sync", lambda s, tc=tc: s.dma_start(
                out=self.bf(SN + 1024 * tc, 1024), in_=self.rscr[:, 128 * tc:128 * (tc + 1)], transpose=True),
                after=[self.r_store], sig="SQ", dma=True))
        self.tap("rtok", self.bf(SN, 16384), [e_rt[15]])
        e_wv2t = p.step("gpsimd", lambda g: g.dma_start(
            out=self.bf(PH, 8192).rearrange("p (kc c) -> p kc c", kc=8),
            in_=self.wv2t.rearrange("(kc p) c -> p kc c", p=128)), sig="LD", dma=True)
        p.barrier()

        w2out = self.f32(XS + 4096, 1024)
        a1 = self.f32(XS + 4096, 1024)
        e_out_final = None
        prev_gbank = [None, None]
        prev_w2mm = {}
        prev_asm = {}
        for tt in range(4):
            e_gev = {}
            for dff in range(32):
                par = dff % 2
                slot = par
                e_w1 = self.load_w_slice(self.w1p, dff, 128, WS + 2048 * slot, slot)
                bank = self.PBG[:, 1024 * par:1024 * par + 512]
                e_mm = None
                for kc in range(8):
                    e_mm = p.step("tensor", lambda t, dff=dff, kc=kc, tt=tt, bank=bank, slot=slot: t.matmul(
                        bank, self.bf(WS + 2048 * slot + 128 * kc, 128),
                        self.bf(H2 + 2048 * kc + 512 * tt, 512),
                        start=(kc == 0), stop=(kc == 7)),
                        after=[e_w1, e_h2[kc], prev_gbank[par]] + ([prev_asm.get(tt - 1)] if kc == 0 else []),
                        sig="TC" if kc == 7 else None)
                self.ws_user[slot] = e_mm
                e_g = p.step("vector", lambda v, dff=dff, bank=bank: v.tensor_scalar(
                    self.bf(QD + 512 * dff, 512), bank, self.CF[:, 48 + dff:49 + dff], 0.0, ALU.add, ALU.max),
                    after=[e_mm], sig="VC")
                prev_gbank[par] = e_g
                e_gev[dff] = e_g
            banks = [self.PBG[:, 512 * i:512 * (i + 1)] for i in range(4)] + [self.P4, self.P5, self.P6, self.P7]
            e_w2mm = {}
            for dff in range(32):
                slot2 = 2
                e_w2 = p.step("gpsimd", lambda g, dff=dff: g.dma_start(
                    out=self.bf(WS + 4096, 1024), in_=self.w2[128 * dff:128 * (dff + 1), :]),
                    after=[self.ws_user[2]], sig="LD", dma=True)
                for tl in range(4):
                    for dh in range(2):
                        bk = banks[tl + 4 * dh]
                        e_w2mm[(tl, dh)] = p.step("tensor", lambda t, dff=dff, tl=tl, dh=dh, bk=bk: t.matmul(
                            bk[:, 0:512],
                            self.bf(QD + 512 * dff + 128 * tl, 128),
                            self.bf(WS + 4096 + 512 * dh, 512),
                            start=(dff == 0), stop=(dff == 31)),
                            after=[e_w2, e_gev[dff]] + ([prev_w2mm.get((tl, dh)), e_gev[31]] if dff == 0 else []),
                            sig="TC" if (tl == 3 and dh == 1) else None)
                self.ws_user[2] = e_w2mm[(3, 1)]
            if tt == 0 and "gsb2" in TAPS:
                tg1 = self.nc.dram_tensor("dbg_gsb2", [128, 16384], BF16, kind="ExternalOutput").ap()
                p.step("sync", lambda s, tg1=tg1: s.dma_start(out=tg1, in_=self.bf(QD, 16384)),
                       after=[e_w2mm[(3, 1)]], sig="SQ", dma=True)
                tg2 = self.nc.dram_tensor("dbg_w2sl", [128, 1024], BF16, kind="ExternalOutput").ap()
                p.step("sync", lambda s, tg2=tg2: s.dma_start(out=tg2, in_=self.bf(WS + 4096, 1024)),
                       after=[e_w2mm[(3, 1)]], sig="SQ", dma=True)
                e_bc0 = p.step("vector", lambda v: v.tensor_copy(self.f32(XS + 8192, 512), banks[0][:, 0:512]),
                               after=[e_w2mm[(3, 1)]], sig="VC")
                tg3 = self.nc.dram_tensor("dbg_bank0", [128, 512], F32, kind="ExternalOutput").ap()
                p.step("sync", lambda s, tg3=tg3: s.dma_start(out=tg3, in_=self.f32(XS + 8192, 512)),
                       after=[e_bc0], sig="SQ", dma=True)
            if tt == 0:
                self.tap("gsb", self.bf(QD, 16384), [e_gev[31]])
            for tl in range(4):
                tc = 4 * tt + tl
                bk0 = banks[tl]
                bk1 = banks[tl + 4]
                e_e0 = p.step("vector", lambda v, bk0=bk0: v.tensor_tensor(
                    w2out[:, 0:512], bk0[:, 0:512], self.C2[:, 1056:1568], ALU.add),
                    after=[e_w2mm[(3, 1)], prev_asm.get(tt)], sig="VC")
                e_e1 = p.step("vector", lambda v, bk1=bk1: v.tensor_tensor(
                    w2out[:, 512:1024], bk1[:, 0:512], self.C2[:, 1568:2080], ALU.add),
                    after=[e_e0], sig="VC")
                if tt == 0 and tl == 0:
                    self.tap("w2o", w2out, [e_e1])
                e_v0 = None
                e_v1 = None
                for dh in range(2):
                    bk = bk0 if dh == 0 else bk1
                    e_mmv = None
                    for kc in range(8):
                        e_mmv = p.step("tensor", lambda t, tc=tc, dh=dh, kc=kc, bk=bk: t.matmul(
                            bk[:, 0:512],
                            self.bf(XN + 2048 * kc + 128 * tc, 128),
                            self.bf(PH + 1024 * kc + 512 * dh, 512),
                            start=(kc == 0), stop=(kc == 7)),
                            after=[e_wv2t, e_e1 if dh == 0 else None, e_e1 if dh == 1 else None],
                            sig="TC" if kc == 7 else None)
                    if dh == 0:
                        e_v0 = e_mmv
                    else:
                        e_v1 = e_mmv
                xio = self.f32(XS + 6144 + 2048 * (tl % 2), 1024)
                e_x = p.step("sync", lambda s, tc=tc, xio=xio: s.dma_start(
                    out=xio, in_=self.x_tok[128 * tc:128 * (tc + 1), :]),
                    after=[prev_asm.get((tt, tl % 2))], sig="SQ", dma=True)
                e_p1 = p.step("gpsimd", lambda g, xio=xio: g.tensor_tensor(xio, xio, w2out, ALU.add),
                              after=[e_x, e_e1], sig="PC")
                e_a0 = p.step("vector", lambda v, bk0=bk0, tc=tc: v.tensor_tensor(
                    a1[:, 0:512], bk0[:, 0:512], self.bf(SN + 1024 * tc, 512), ALU.mult),
                    after=[e_v0, e_rt[tc], e_e1, e_p1], sig="VC")
                e_a1 = p.step("vector", lambda v, bk1=bk1, tc=tc: v.tensor_tensor(
                    a1[:, 512:1024], bk1[:, 0:512], self.bf(SN + 1024 * tc + 512, 512), ALU.mult),
                    after=[e_v1, e_a0], sig="VC")
                if tt == 0 and tl == 0:
                    self.tap("a1t", a1, [e_a1])
                e_p2 = p.step("vector", lambda v, xio=xio: v.tensor_tensor(xio, xio, a1, ALU.add),
                              after=[e_p1, e_a1], sig="VC")
                e_st = p.step("sync", lambda s, tc=tc, xio=xio: s.dma_start(
                    out=self.out[128 * tc:128 * (tc + 1), :], in_=xio),
                    after=[e_p2], sig="SQ", dma=True)
                prev_asm[(tt, tl % 2)] = e_st
                prev_asm[tt] = e_a1
                prev_w2mm[(tl, 0)] = e_a0
                prev_w2mm[(tl, 1)] = e_a1
                e_out_final = e_st
            p.barrier()
        p.step("sync", lambda s: s.nop(), after=[e_out_final])
        return self


def build_program():
    b = B()
    b.kk_bank_user = {}
    b.kk_tmp_user = None
    b.r_store = None
    b.build()
    nc = b.nc
    sems = {}
    import contextlib
    stack = contextlib.ExitStack()
    for name in b.p.count:
        sems[name] = stack.enter_context(nc.semaphore(name))
    with nc.Block() as block:
        block.tensor(lambda t: b.p.emit_engine("tensor", t, sems))
        block.vector(lambda v: b.p.emit_engine("vector", v, sems))
        block.scalar(lambda s: b.p.emit_engine("scalar", s, sems))
        block.gpsimd(lambda g: b.p.emit_engine("gpsimd", g, sems))
        block.sync(lambda s: b.p.emit_engine("sync", s, sems))
    stack.close()
    return nc


_NC_CACHE = None
_LAST_RES = None


def kernel(**inputs):
    global _NC_CACHE
    x = np.asarray(inputs["L0_attn_batch"])
    wq = np.asarray(inputs["L0_attn_Wq"])
    wk = np.asarray(inputs["L0_attn_Wk"])
    wv = np.asarray(inputs["L0_attn_Wv"])
    wo = np.asarray(inputs["L0_attn_Wo"])
    omega = np.asarray(inputs["L0_attn_omega"])
    w1 = np.asarray(inputs["L0_attn_W1"])
    b1 = np.asarray(inputs["L0_attn_b1"])
    w2 = np.asarray(inputs["L0_attn_W2"])
    b2 = np.asarray(inputs["L0_attn_b2"])
    gamma = np.asarray(inputs["L0_attn_gamma"])
    beta = np.asarray(inputs["L0_attn_beta"])
    wr = np.asarray(inputs["L0_ffn_W_r"])
    wk2 = np.asarray(inputs["L0_ffn_W_k"])
    wv2 = np.asarray(inputs["L0_ffn_W_v"])
    mur = np.asarray(inputs["L0_ffn_mu_r"])
    muk = np.asarray(inputs["L0_ffn_mu_k"])

    def pack_lhst(w, ncols):
        nsl = w.shape[1] // ncols
        a = w.reshape(8, 128, nsl, ncols)
        a = np.transpose(a, (2, 1, 0, 3))
        return np.ascontiguousarray(a.reshape(nsl * 128, 8 * ncols))

    com = {
        "wqp": pack_lhst(wq, 256), "wkp": pack_lhst(wk, 256), "wv": wv,
        "wop": pack_lhst(wo, 128),
        "womega_t": np.ascontiguousarray(omega.T),
        "w1p": pack_lhst(w1, 128), "b1_t": np.ascontiguousarray(b1.reshape(32, 128).T),
        "w2": w2, "b2row": b2.reshape(1, 1024),
        "gamma_t": np.ascontiguousarray(gamma.reshape(8, 128).T),
        "beta_t": np.ascontiguousarray(beta.reshape(8, 128).T),
        "wrtp": pack_lhst(np.ascontiguousarray(wr.T), 128),
        "wk2p": pack_lhst(np.ascontiguousarray(wk2.T), 128),
        "wv2t": np.ascontiguousarray(wv2.T),
        "mur_t": np.ascontiguousarray(mur.reshape(8, 128).T),
        "muk_t": np.ascontiguousarray(muk.reshape(8, 128).T),
        "utmask": np.triu(np.ones((128, 128), np.float32)),
    }
    com = {k: np.ascontiguousarray(v, dtype=np.float32) for k, v in com.items()}
    zrow = np.zeros((128, 8), np.float32)
    zpre = np.zeros((2048, 1024), np.float32)
    in_maps = []
    for core in range(8):
        bb, half = core // 2, core % 2
        m = dict(com)
        m["x_tok"] = np.ascontiguousarray(x[bb, 2048 * half:2048 * (half + 1)])
        m["xp_tok"] = np.ascontiguousarray(x[bb, 0:2048]) if half == 1 else zpre
        m["prevrow_fm"] = (np.ascontiguousarray(x[bb, 2047].reshape(8, 128).T)
                           if half == 1 else zrow)
        m["smask"] = np.full((128, 1), float(half), np.float32)
        in_maps.append(m)

    if _NC_CACHE is None:
        _NC_CACHE = build_program()
    res = run_bass_kernel_spmd(_NC_CACHE, in_maps, core_ids=list(range(8)))
    global _LAST_RES
    _LAST_RES = res
    out = np.empty((4, 4096, 1024), np.float32)
    for core in range(8):
        bb, half = core // 2, core % 2
        out[bb, 2048 * half:2048 * (half + 1)] = res.results[core]["out"]
    return out


# version 07
# speedup vs baseline: 1.2818x; 1.2818x over previous
import sys
if '/opt/trn_rl_repo' not in sys.path:
    sys.path.insert(0, '/opt/trn_rl_repo')
import math
import numpy as np
import ml_dtypes
import concourse.bass as bass
import concourse.mybir as mybir
from concourse.bass_utils import run_bass_kernel_spmd

F32 = mybir.dt.float32
BF16 = mybir.dt.bfloat16
AF = mybir.ActivationFunctionType
ALU = mybir.AluOpType
AX = mybir.AxisListType

T = 2048
NCH = 16
D = 1024
SCALE = 64.0 ** -0.25
FEPS = 1e-6
SQM = float(math.sqrt(128.0))

ENGINES = ("tensor", "vector", "scalar", "gpsimd", "sync")
TAPS = set()
TAPROT = -1

XN = 0
VT = 16384
SN = 33280
QD = 50176
PH = 66560
WS = 87040
XS = 92160
ACOLS = 102400

H2 = VT


class Prog:
    def __init__(self):
        self.steps = {e: [] for e in ENGINES}
        self.count = {s: 0 for s in ("LD", "SQ", "AQ", "TC", "VC", "SC", "PC")}
        self.pending = {e: () for e in ENGINES}

    def step(self, eng, fn, after=(), sig=None, dma=False):
        after = tuple(e for e in after if e is not None) + self.pending[eng]
        self.pending[eng] = ()
        incs = ()
        ev = None
        if sig is not None:
            n = 16 if dma else 1
            self.count[sig] += n
            ev = (sig, self.count[sig])
            incs = ((sig, n),)
        self.steps[eng].append((fn, after, incs))
        return ev

    def barrier(self):
        evs = tuple((s, v) for s, v in self.count.items() if v > 0)
        for e in ENGINES:
            self.pending[e] = self.pending[e] + evs

    def emit_engine(self, eng_name, eng, sems):
        seen = {}
        for fn, waits, incs in self.steps[eng_name]:
            for (s, v) in waits:
                if seen.get(s, 0) < v:
                    eng.wait_ge(sems[s], v)
                    seen[s] = v
            inst = fn(eng)
            for (s, n) in incs:
                inst.then_inc(sems[s], n)


class B:
    def __init__(self):
        nc = bass.Bass("TRN2", target_bir_lowering=False)
        self.nc = nc
        self.p = Prog()

        def inp(name, shape):
            return nc.dram_tensor(name, list(shape), F32, kind="ExternalInput").ap()

        self.x_tok = inp("x_tok", (T, D))
        self.xp_tok = inp("xp_tok", (T, D))
        self.prevrow = inp("prevrow_fm", (128, 8))
        self.smask = inp("smask", (128, 1))
        self.wqp = inp("wqp", (512, 2048))
        self.wkp = inp("wkp", (512, 2048))
        self.wv = inp("wv", (D, D))
        self.wop = inp("wop", (1024, 1024))
        self.womega_t = inp("womega_t", (64, 128))
        self.w1p = inp("w1p", (4096, 1024))
        self.b1_t = inp("b1_t", (128, 32))
        self.w2 = inp("w2", (4096, D))
        self.b2row = inp("b2row", (1, D))
        self.gamma_t = inp("gamma_t", (128, 8))
        self.beta_t = inp("beta_t", (128, 8))
        self.wrtp = inp("wrtp", (1024, 1024))
        self.wk2p = inp("wk2p", (1024, 1024))
        self.wv2t = inp("wv2t", (D, D))
        self.mur_t = inp("mur_t", (128, 8))
        self.muk_t = inp("muk_t", (128, 8))
        self.utmask_in = inp("utmask", (128, 128))

        self.out = nc.dram_tensor("out", [T, D], F32, kind="ExternalOutput").ap()

        self.scrA = nc.dram_tensor("scrA", [T, D], BF16, kind="Internal").ap()
        self.wb1 = nc.dram_tensor("wb1", [4096, 1024], BF16, kind="Internal").ap()
        self.wb2 = nc.dram_tensor("wb2", [4096, 1024], BF16, kind="Internal").ap()
        self.rscr = nc.dram_tensor("rscr", [D, T], BF16, kind="Internal").ap()
        self.NROT = 4
        self.pscr = [nc.dram_tensor(f"pscr{i}", [T, 128], BF16, kind="Internal").ap() for i in range(self.NROT)]
        self.qscr = [nc.dram_tensor(f"qscr{i}", [T, 128], BF16, kind="Internal").ap() for i in range(self.NROT)]
        self.pscr_free = [None] * self.NROT
        self.qscr_free = [None] * self.NROT

        self.arena = nc.alloc_sbuf_tensor("arena", [128, ACOLS], BF16)
        self.A = self.arena.ap()
        self.cst = nc.alloc_sbuf_tensor("cst", [128, 516], BF16)
        self.C = self.cst.ap()
        self.cst2 = nc.alloc_sbuf_tensor("cst2", [128, 2120], BF16)
        self.C2 = self.cst2.ap()
        self.cstf = nc.alloc_sbuf_tensor("cstf", [128, 480], F32)
        self.CF = self.cstf.ap()

        self.pbig = nc.alloc_psum_tensor("pbig", [128, 2048], F32)
        self.PBG = self.pbig.ap()
        self.P4 = nc.alloc_psum_tensor("ps4", [128, 512], F32).ap()
        self.P5 = nc.alloc_psum_tensor("ps5", [128, 512], F32).ap()
        self.P6 = nc.alloc_psum_tensor("ps6", [128, 512], F32).ap()
        self.P7 = nc.alloc_psum_tensor("ps7", [128, 512], F32).ap()

        self.phi_pbig = [None, None]
        self.proj_res = {}
        self.phi_stats_user = None
        self.ws_user = [None, None, None]
        self.qtk_user = None
        self.pM_user = [None, None]
        self.ptk_user = [None, None]
        self.stats_user = [None, None]
        self.kk_bank_user = {}
        self.kk_tmp_user = None
        self.r_store = None

    def tap(self, tag, ap, after):
        if tag not in TAPS:
            return
        t = self.nc.dram_tensor("dbg_" + tag, list(ap.shape), ap.dtype, kind="ExternalOutput").ap()
        self.p.step("sync", lambda s, t=t, ap=ap: s.dma_start(out=t, in_=ap), after=after, sig="SQ", dma=True)
        self.p.barrier()

    def bf(self, c0, n):
        return self.A[:, c0:c0 + n]

    def f32(self, c0, n):
        assert c0 % 2 == 0
        return self.A.bitcast(F32)[:, c0 // 2: c0 // 2 + n]

    def consts(self):
        p, C, C2, CF = self.p, self.C, self.C2, self.CF
        p.step("gpsimd", lambda g: g.dma_start(out=C[:, 0:128], in_=self.utmask_in), sig="LD", dma=True)
        p.step("gpsimd", lambda g: g.dma_start(out=C[0:64, 260:388], in_=self.womega_t), sig="LD", dma=True)
        p.step("gpsimd", lambda g: g.dma_start(out=C[64:128, 260:388], in_=self.womega_t), sig="LD", dma=True)
        p.step("gpsimd", lambda g: g.dma_start(out=self.bf(XS, 1024)[0:1, :], in_=self.b2row), sig="LD", dma=True)
        for (ap, c0) in ((self.gamma_t, 0), (self.beta_t, 8), (self.mur_t, 16), (self.muk_t, 24),
                         (self.b1_t, 48), (self.smask, 80), (self.prevrow, 136)):
            n = ap.shape[1]
            p.step("gpsimd", lambda g, ap=ap, c0=c0, n=n: g.dma_start(out=CF[:, c0:c0 + n], in_=ap), sig="LD", dma=True)
        e_cd = ("LD", p.count["LD"])

        p.step("vector", lambda v: v.memset(C[:, 388:516], -0.5), sig="VC")
        p.step("vector", lambda v: v.memset(CF[:, 146:147], 1e-5), sig="VC")
        p.step("vector", lambda v: v.memset(CF[:, 147:148], 1e6), sig="VC")
        p.step("vector", lambda v: v.memset(C[:, 128:129], 1.0), sig="VC")
        e_ones = p.step("vector", lambda v: v.memset(C[0:1, 129:257], 1.0), sig="VC")
        self.e_ones = e_ones
        p.step("vector", lambda v: v.tensor_scalar(CF[:, 32:40], CF[:, 16:24], -1.0, 1.0, ALU.mult, ALU.add), after=[e_cd], sig="VC")
        p.step("vector", lambda v: v.tensor_scalar(CF[:, 40:48], CF[:, 24:32], -1.0, 1.0, ALU.mult, ALU.add), sig="VC")
        for i, pb in ((0, self.P4), (1, self.P5)):
            e_mm = p.step("tensor", lambda t, i=i, pb=pb: t.matmul(
                pb[:, 0:512], C[0:1, 129:257], self.bf(XS, 1024)[0:1, 512 * i:512 * (i + 1)], start=True, stop=True),
                after=[e_cd, e_ones], sig="TC")
            p.step("vector", lambda v, i=i, pb=pb: v.tensor_copy(
                C2[:, 1056 + 512 * i:1056 + 512 * (i + 1)], pb[:, 0:512]), after=[e_mm], sig="VC")

        RW = lambda i: self.f32(PH + 16384, 64)[0:1, i:i + 1]
        e_pb = p.step("vector", lambda v: v.tensor_copy(C2[:, 2080:2088], CF[:, 136:144]), after=[e_cd], sig="VC")
        e_pq = p.step("vector", lambda v: v.tensor_tensor(C2[:, 2088:2096], C2[:, 2080:2088], C2[:, 2080:2088], ALU.mult), after=[e_pb], sig="VC")
        e_s1 = p.step("tensor", lambda t: t.matmul(self.P6[0:1, 0:8], C[:, 128:129], C2[:, 2080:2088], start=True, stop=True), after=[e_pb, e_ones], sig="TC")
        e_s2 = p.step("tensor", lambda t: t.matmul(self.P6[0:1, 8:16], C[:, 128:129], C2[:, 2088:2096], start=True, stop=True), after=[e_pq], sig="TC")
        e_r1 = p.step("vector", lambda v: v.tensor_reduce(RW(0), self.P6[0:1, 0:8], AX.X, ALU.add), after=[e_s1], sig="VC")
        e_r2 = p.step("vector", lambda v: v.tensor_reduce(RW(1), self.P6[0:1, 8:16], AX.X, ALU.add), after=[e_s2], sig="VC")
        e_m = p.step("vector", lambda v: v.tensor_scalar(RW(2), RW(0), 1.0 / 1024.0, None, ALU.mult), after=[e_r1], sig="VC")
        e_m2 = p.step("vector", lambda v: v.tensor_tensor(RW(3), RW(2), RW(2), ALU.mult), after=[e_m], sig="VC")
        e_va = p.step("vector", lambda v: v.scalar_tensor_tensor(RW(4), RW(1), 1.0 / 1024.0, RW(3), ALU.mult, ALU.subtract), after=[e_m2, e_r2], sig="VC")
        e_ln = p.step("scalar", lambda s: s.activation(RW(5), RW(4), AF.Ln, bias=CF[0:1, 146:147]), after=[e_va], sig="SC")
        e_rs = p.step("scalar", lambda s: s.activation(RW(6), RW(5), AF.Exp, scale=-0.5), after=[e_ln], sig="SC")
        bft = lambda i: C2[0:1, 2104 + i:2105 + i]
        e_c1 = p.step("vector", lambda v: v.tensor_copy(bft(0), RW(2)), after=[e_m], sig="VC")
        e_c2 = p.step("vector", lambda v: v.tensor_copy(bft(1), RW(6)), after=[e_rs], sig="VC")
        e_b1 = p.step("tensor", lambda t: t.matmul(self.P6[:, 20:21], C[0:1, 129:257], bft(0), start=True, stop=True), after=[e_c1], sig="TC")
        e_b2 = p.step("tensor", lambda t: t.matmul(self.P6[:, 21:22], C[0:1, 129:257], bft(1), start=True, stop=True), after=[e_c2], sig="TC")
        e_muc = p.step("vector", lambda v: v.tensor_copy(CF[:, 144:145], self.P6[:, 20:21]), after=[e_b1], sig="VC")
        e_rsc = p.step("vector", lambda v: v.tensor_copy(CF[:, 145:146], self.P6[:, 21:22]), after=[e_b2], sig="VC")
        p.step("vector", lambda v: v.tensor_scalar(
            C2[:, 2096:2104], CF[:, 136:144], CF[:, 144:145], CF[:, 145:146], ALU.subtract, ALU.mult),
            after=[e_muc, e_rsc], sig="VC")
        self.e_utmask = e_cd
        self.p.barrier()

    def ln_tok(self, x_in):
        p, CF = self.p, self.CF
        xt_cols = [PH, PH + 2048]
        xn_cols = [PH + 4096, PH + 5120]
        sq_cols = PH + 6144
        last_xt_reader = [None, None]
        prev_store = [None, None]
        prev_sq = None
        last_stats = [None, None]
        e_store = []
        for j in range(NCH):
            par = j % 2
            xt = self.f32(xt_cols[par], 1024)
            xn = self.bf(xn_cols[par], 1024)
            sb = 88 + 8 * par
            e_ld = p.step("sync", lambda s, j=j, xt=xt: s.dma_start(out=xt, in_=x_in[128 * j:128 * (j + 1), :]),
                          after=[last_xt_reader[par]], sig="SQ", dma=True)
            e_sum = p.step("vector", lambda v, xt=xt, sb=sb: v.reduce_sum(CF[:, sb:sb + 1], xt, axis=AX.X),
                           after=[e_ld, last_stats[par]], sig="VC")
            e_sq = p.step("scalar", lambda s, xt=xt: s.activation(self.f32(sq_cols, 1024), xt, AF.Square),
                          after=[e_ld, prev_sq], sig="SC")
            e_sum2 = p.step("vector", lambda v, sb=sb: v.reduce_sum(CF[:, sb + 1:sb + 2], self.f32(sq_cols, 1024), axis=AX.X),
                            after=[e_sq], sig="VC")
            prev_sq = e_sum2
            e_mu = p.step("vector", lambda v, sb=sb: v.tensor_scalar(CF[:, sb + 2:sb + 3], CF[:, sb:sb + 1], -1.0 / 1024.0, None, ALU.mult),
                          after=[e_sum], sig="VC")
            e_mu2 = p.step("vector", lambda v, sb=sb: v.tensor_tensor(CF[:, sb + 3:sb + 4], CF[:, sb + 2:sb + 3], CF[:, sb + 2:sb + 3], ALU.mult),
                           after=[e_mu], sig="VC")
            e_var = p.step("vector", lambda v, sb=sb: v.scalar_tensor_tensor(
                CF[:, sb + 4:sb + 5], CF[:, sb + 1:sb + 2], 1.0 / 1024.0, CF[:, sb + 3:sb + 4], ALU.mult, ALU.subtract),
                after=[e_mu2, e_sum2], sig="VC")
            e_lnr = p.step("scalar", lambda s, sb=sb: s.activation(CF[:, sb + 5:sb + 6], CF[:, sb + 4:sb + 5], AF.Ln, bias=CF[:, 146:147]),
                           after=[e_var], sig="SC")
            e_rstd = p.step("scalar", lambda s, sb=sb: s.activation(CF[:, sb + 6:sb + 7], CF[:, sb + 5:sb + 6], AF.Exp, scale=-0.5),
                            after=[e_lnr], sig="SC")
            e_xn = p.step("vector", lambda v, xt=xt, xn=xn, sb=sb: v.tensor_scalar(
                xn, xt, CF[:, sb + 2:sb + 3], CF[:, sb + 6:sb + 7], ALU.add, ALU.mult),
                after=[e_rstd, prev_store[par]], sig="VC")
            last_xt_reader[par] = e_xn
            last_stats[par] = e_xn
            ev = p.step("scalar", lambda s, j=j, xn=xn: s.dma_start(out=self.scrA[128 * j:128 * (j + 1), :], in_=xn),
                        after=[e_xn], sig="AQ", dma=True)
            prev_store[par] = ev
            e_store.append(ev)
        e_all = e_store[-1]
        e_tl = []
        for c in range(8):
            e_tl.append(p.step("sync", lambda s, c=c: s.dma_start(
                out=self.bf(XN + 2048 * c, 2048), in_=self.scrA[:, 128 * c:128 * (c + 1)], transpose=True),
                after=[e_all], sig="SQ", dma=True))
        return e_tl

    def v_proj(self, xn_tl, first):
        p = self.p
        e_wv = p.step("gpsimd", lambda g: g.dma_start(
            out=self.bf(QD, 8192).rearrange("p (kc c) -> p kc c", kc=8),
            in_=self.wv.rearrange("(kc p) c -> p kc c", p=128)), sig="LD", dma=True)
        e_ones = None
        if first:
            for j in range(NCH):
                ap = self.A[:, VT + 1056 * j + 64: VT + 1056 * j + 64 + 16 * 66:66]
                e_ones = p.step("gpsimd", lambda g, ap=ap: g.memset(ap, 1.0), sig="PC")
        evs = []
        prev_evac = [None, None]
        for j in range(NCH):
            par = j % 2
            pb = self.PBG[:, 1024 * par:1024 * (par + 1)]
            e_mm = None
            for dh in range(2):
                for kc in range(8):
                    e_mm = p.step("tensor", lambda t, j=j, dh=dh, kc=kc, pb=pb: t.matmul(
                        pb[:, 512 * dh:512 * (dh + 1)],
                        self.bf(XN + 2048 * kc + 128 * j, 128),
                        self.bf(QD + 1024 * kc + 512 * dh, 512),
                        start=(kc == 0), stop=(kc == 7)),
                        after=[e_wv, xn_tl[kc], prev_evac[par]],
                        sig="TC" if (dh == 1 and kc == 7) else None)
            dst = self.A[:, VT + 1056 * j: VT + 1056 * j + 16 * 66].rearrange("p (h c) -> p h c", h=16)[:, :, 0:64]
            src = pb.rearrange("p (h c) -> p h c", h=16)
            ev = p.step("vector", lambda v, dst=dst, src=src: v.tensor_copy(dst, src),
                        after=[e_mm, e_ones], sig="VC")
            prev_evac[par] = ev
            evs.append(ev)
        return evs

    def load_w_slice(self, w_packed, idx, ncols, slot_cols, slot_id, after=()):
        src = w_packed[128 * idx:128 * (idx + 1), :]
        dst = self.bf(slot_cols, 8 * ncols)
        ev = self.p.step("gpsimd", lambda g: g.dma_start(out=dst, in_=src),
                         after=list(after) + [self.ws_user[slot_id]], sig="LD", dma=True)
        return ev

    def proj_pair_gen(self, w_ev, wslot, slot_id, xn_tl, scale, base):
        p = self.p
        e_list = []
        for ci in range(2):
            for tt in range(4):
                bi = (4 * ci + tt) % 4
                bank = self.PBG[:, 512 * bi: 512 * (bi + 1)]
                idx = 4 * ci + tt
                e_mm = None
                for kc in range(8):
                    e_mm = p.step("tensor", lambda t, ci=ci, tt=tt, kc=kc, bank=bank, wslot=wslot: t.matmul(
                        bank, self.bf(wslot + 256 * kc + 128 * ci, 128),
                        self.bf(XN + 2048 * kc + 512 * tt, 512),
                        start=(kc == 0), stop=(kc == 7)),
                        after=[w_ev, xn_tl[kc]] + ([e_list[idx - 4][0]] if idx >= 4 else [])
                        + [self.phi_pbig[0], self.phi_pbig[1]],
                        sig="TC" if kc == 7 else None)
                e_e = p.step("scalar", lambda s, ci=ci, tt=tt, bank=bank, base=base: s.activation(
                    self.bf(base + 2048 * ci + 512 * tt, 512), bank, AF.Copy, scale=scale),
                    after=[e_mm], sig="SC")
                e_q = p.step("vector", lambda v, ci=ci, tt=tt, base=base: v.tensor_tensor(
                    self.bf(base + 4096 + 2048 * ci + 512 * tt, 512),
                    self.bf(base + 2048 * ci + 512 * tt, 512),
                    self.bf(base + 2048 * ci + 512 * tt, 512), ALU.mult),
                    after=[e_e], sig="VC")
                e_list.append((e_e, e_q))
                self.ws_user[slot_id] = e_mm
                self.proj_res[base] = (e_e, e_q)
                yield

    def proj_pair(self, w_ev, wslot, slot_id, xn_tl, scale, base):
        for _ in self.proj_pair_gen(w_ev, wslot, slot_id, xn_tl, scale, base):
            pass
        return self.proj_res[base]

    def phi(self, h, out_tok_cols, outM_cols, rot, want_M, after_proj, src_base):
        p, C, CF = self.p, self.C, self.CF
        odd = h % 2
        ci = (h // 2) % 2
        r0 = 64 * odd
        par2 = rot % 2
        pM = PH + 16384 if par2 == 0 else XS + 2048
        ptk = PH + 18432 if par2 == 0 else XS + 4096
        qtk = pM
        mx0, sr0 = (104, 120) if par2 == 0 else (440, 456)
        otok = out_tok_cols if out_tok_cols is not None else qtk
        rr = rot % self.NROT

        e_exp = []
        for half in range(2):
            pb = self.PBG[:, 1024 * half:1024 * (half + 1)]
            e_mm = None
            for blk in range(2):
                cols = 1024 * half + 512 * blk
                p.step("tensor", lambda t, pb=pb, blk=blk, cols=cols, src_base=src_base: t.matmul(
                    pb[:, 512 * blk:512 * (blk + 1)],
                    C[r0:r0 + 64, 260:388],
                    self.A[r0:r0 + 64, src_base + 2048 * ci + cols: src_base + 2048 * ci + cols + 512],
                    start=True, stop=False),
                    after=list(after_proj) + [self.phi_pbig[half]], sig=None)
                e_mm = p.step("tensor", lambda t, pb=pb, blk=blk, cols=cols, src_base=src_base: t.matmul(
                    pb[:, 512 * blk:512 * (blk + 1)],
                    C[r0:r0 + 64, 388:516],
                    self.A[r0:r0 + 64, src_base + 4096 + 2048 * ci + cols: src_base + 4096 + 2048 * ci + cols + 512],
                    start=False, stop=True), sig="TC")
            e_exp.append(p.step("scalar", lambda s, half=half, pb=pb, pM=pM: s.activation(
                self.bf(pM + 1024 * half, 1024), pb, AF.Exp),
                after=[e_mm, self.pM_user[par2]], sig="SC"))
            self.phi_pbig[half] = e_exp[half]
        if h == 1 and want_M and out_tok_cols is not None:
            self.tap("pm1", self.bf(pM, 2048), e_exp)
        pd = self.pscr[rr]
        o3 = pd.rearrange("(c m) p -> c m p", m=128).transpose([1, 0, 2])
        i3 = self.bf(pM, 2048).rearrange("m (c q) -> m c q", q=128)
        e_st = p.step("scalar", lambda s, o3=o3, i3=i3: s.dma_start(out=o3, in_=i3),
                      after=e_exp + [self.pscr_free[rr]], sig="AQ", dma=True)
        e_tlp = p.step("sync", lambda s, pd=pd, ptk=ptk: s.dma_start(out=self.bf(ptk, 2048), in_=pd, transpose=True),
                       after=[e_st, self.ptk_user[par2]], sig="SQ", dma=True)
        self.pscr_free[rr] = None
        p3 = self.bf(ptk, 2048).rearrange("p (c m) -> p c m", m=128)
        e_mx = p.step("vector", lambda v, p3=p3, mx0=mx0: v.tensor_reduce(CF[:, mx0:mx0 + 16], p3, AX.X, ALU.max),
                      after=[e_tlp, self.stats_user[par2]], sig="VC")
        e_s1 = p.step("vector", lambda v, mx0=mx0, sr0=sr0: v.tensor_scalar(CF[:, sr0:sr0 + 16], CF[:, mx0:mx0 + 16], SQM, None, ALU.mult),
                      after=[e_mx], sig="VC")
        e_s2 = p.step("vector", lambda v, sr0=sr0: v.reciprocal(CF[:, sr0:sr0 + 16], CF[:, sr0:sr0 + 16]), after=[e_s1], sig="VC")
        sb3 = CF[:, sr0:sr0 + 16][:, :, None].broadcast_to([128, 16, 128])
        ot3 = self.bf(otok, 2048).rearrange("p (c m) -> p c m", m=128)
        e_t1 = p.step("vector", lambda v, ot3=ot3, p3=p3, sb3=sb3: v.tensor_tensor(ot3, p3, sb3, ALU.mult),
                      after=[e_s2] + ([self.qtk_user] if out_tok_cols is None else []), sig="VC")
        self.pscr_free[rr] = e_t1
        self.ptk_user[par2] = e_t1
        self.stats_user[par2] = e_t1
        e_tok = p.step("vector", lambda v, otok=otok: v.tensor_scalar(
            self.bf(otok, 2048), self.bf(otok, 2048), FEPS, None, ALU.add),
            after=[e_t1], sig="VC")
        if h == 1 and want_M and out_tok_cols is not None:
            self.tap("ptok1", self.bf(ptk, 2048), [e_t1])
            self.tap("ktok1", self.bf(otok, 2048), [e_tok])
        e_M = None
        if want_M:
            qd = self.qscr[rr]
            o3b = qd.rearrange("(c q) m -> c q m", q=128).transpose([1, 0, 2])
            i3b = self.bf(otok, 2048).rearrange("q (c m) -> q c m", m=128)
            if rot == TAPROT:
                tq1 = self.nc.dram_tensor("dbg_qtok", [128, 2048], BF16, kind="ExternalOutput").ap()
                p.step("sync", lambda s, tq1=tq1, otok=otok: s.dma_start(out=tq1, in_=self.bf(otok, 2048)),
                       after=[e_tok], sig="SQ", dma=True)
            e_st2 = p.step("gpsimd", lambda g, o3b=o3b, i3b=i3b: g.dma_start(out=o3b, in_=i3b),
                           after=[e_tok, self.qscr_free[rr]], sig="LD", dma=True)
            if rot == TAPROT:
                tq2 = self.nc.dram_tensor("dbg_qscr", [T, 128], BF16, kind="ExternalOutput").ap()
                p.step("sync", lambda s, tq2=tq2, qd=qd: s.dma_start(out=tq2, in_=qd),
                       after=[e_st2], sig="SQ", dma=True)
                tq3 = self.nc.dram_tensor("dbg_ptok2", [128, 2048], BF16, kind="ExternalOutput").ap()
                p.step("sync", lambda s, tq3=tq3: s.dma_start(out=tq3, in_=self.bf(PH + 18432, 2048)),
                       after=[e_tok], sig="SQ", dma=True)
            e_M = p.step("sync", lambda s, qd=qd: s.dma_start(out=self.bf(outM_cols, 2048), in_=qd, transpose=True),
                         after=[e_st2], sig="SQ", dma=True)
            self.qscr_free[rr] = e_M
            if out_tok_cols is None:
                self.qtk_user = e_st2
                self.pM_user[par2] = e_st2
        if out_tok_cols is not None or e_M is None:
            self.pM_user[par2] = e_st
        return e_tok, e_M

    def build(self):
        p = self.p
        self.consts()
        rot = 0

        self._stg_st = [None, None]
        self.e_wstage = [None, None]

        def stage_weights(wi):
            wsrc, wdst = ((self.w1p, self.wb1), (self.w2, self.wb2))[wi]
            stg = [XS + 6144, XS + 7168]
            for i in range(32):
                par = i % 2
                e_l = p.step("gpsimd", lambda g, wsrc=wsrc, i=i, par=par, stg=stg: g.dma_start(
                    out=self.bf(stg[par], 1024), in_=wsrc[128 * i:128 * (i + 1), :]),
                    after=[self._stg_st[par]], sig="LD", dma=True)
                self._stg_st[par] = p.step("scalar", lambda s, wdst=wdst, i=i, par=par, stg=stg: s.dma_start(
                    out=wdst[128 * i:128 * (i + 1), :], in_=self.bf(stg[par], 1024)),
                    after=[e_l], sig="AQ", dma=True)
            self.e_wstage[wi] = self._stg_st[(32 - 1) % 2]
        self.stage_weights = stage_weights

        xp_tl = self.ln_tok(self.xp_tok)
        vp_evs = self.v_proj(xp_tl, first=True)
        self.stage_weights(0)
        p.barrier()
        e_sinit_last = None
        for g in range(4):
            e_wk = self.load_w_slice(self.wkp, g, 256, WS + 2048 * (g % 2), g % 2)
            pe_e, pe_q = self.proj_pair(e_wk, WS + 2048 * (g % 2), g % 2, xp_tl, SCALE, QD + 8192)
            e_tok4 = []
            for hl in range(4):
                e_tok, _ = self.phi(4 * g + hl, PH + 8192 + 2048 * hl, None, rot, False, [pe_e, pe_q], QD + 8192)
                e_tok4.append(e_tok)
                rot += 1
            for hl in range(4):
                e_mm = None
                for j in range(NCH):
                    e_mm = p.step("tensor", lambda t, hl=hl, j=j, g=g: t.matmul(
                        self.P4[:, 65 * hl:65 * hl + 65],
                        self.bf(PH + 8192 + 2048 * hl + 128 * j, 128),
                        self.A[:, VT + 1056 * j + 66 * (4 * g + hl): VT + 1056 * j + 66 * (4 * g + hl) + 65],
                        start=(j == 0), stop=(j == NCH - 1)),
                        after=([e_tok4[hl], vp_evs[j], e_sinit_last] if j == 0 else [vp_evs[j]]),
                        sig="TC" if j == NCH - 1 else None)
                dst = self.C2[:, 66 * (4 * g + hl): 66 * (4 * g + hl) + 65]
                e_sinit_last = p.step("vector", lambda v, dst=dst, hl=hl: v.tensor_copy(dst, self.P4[:, 65 * hl:65 * hl + 65]),
                                      after=[e_mm], sig="VC")
        e_sim = p.step("vector", lambda v: v.tensor_scalar(
            self.C2[:, 0:1056], self.C2[:, 0:1056], self.CF[:, 80:81], None, ALU.mult),
            after=[e_sinit_last], sig="VC")
        self.tap("sinit", self.C2[:, 0:1056], [e_sim])
        p.barrier()

        x_tl = self.ln_tok(self.x_tok)
        self.tap("xn", self.bf(XN, 16384), [x_tl[7]])
        v_evs = self.v_proj(x_tl, first=False)
        self.stage_weights(1)
        self.tap("vtok", self.bf(VT, 16896), [v_evs[-1]])
        p.barrier()

        e_attn_last = None
        kbase = lambda g: QD + 8192 * (g % 2)
        e_wk0 = self.load_w_slice(self.wkp, 0, 256, WS + 0, 0)
        self.proj_pair(e_wk0, WS + 0, 0, x_tl, SCALE, kbase(0))
        for g in range(4):
            pe_e, pe_q = self.proj_res[kbase(g)]
            e_ktok, e_kM = [], []
            for hl in range(4):
                e_tok, e_M = self.phi(4 * g + hl, PH + 8192 + 2048 * hl, PH + 2048 * hl, rot, True, [pe_e, pe_q], kbase(g))
                e_ktok.append(e_tok)
                e_kM.append(e_M)
                rot += 1
            qslot = (g + 1) % 2
            e_wq = self.load_w_slice(self.wqp, g, 256, WS + 2048 * qslot, qslot)
            qgen = self.proj_pair_gen(e_wq, WS + 2048 * qslot, qslot, x_tl, SCALE, kbase(g))
            e_sr = p.step("vector", lambda v, g=g: v.tensor_copy(
                self.CF[:, 160:424], self.C2[:, 264 * g:264 * g + 264]),
                after=e_ktok, sig="VC")
            e_sn = []
            e_add = e_sr
            for j in range(NCH):
                if j % 2 == 1:
                    next(qgen, None)
                sn0 = SN + 4224 * g
                dst = self.A[:, sn0: sn0 + 4224].rearrange("p (hh c) -> p hh c", hh=4)[:, :, 66 * j:66 * j + 65]
                srun3 = self.CF[:, 160:424].rearrange("p (hh c) -> p hh c", hh=4)[:, :, 0:65]
                e_snap = p.step("vector", lambda v, dst=dst, srun3=srun3: v.tensor_copy(dst, srun3),
                                after=[e_add], sig="VC")
                e_mm = None
                for hl in range(4):
                    e_mm = p.step("tensor", lambda t, hl=hl, j=j, g=g: t.matmul(
                        self.P4[:, 65 * hl:65 * hl + 65],
                        self.bf(PH + 8192 + 2048 * hl + 128 * j, 128),
                        self.A[:, VT + 1056 * j + 66 * (4 * g + hl): VT + 1056 * j + 66 * (4 * g + hl) + 65],
                        start=True, stop=True),
                        after=([e_ktok[hl]] if j == 0 else []) + ([e_add] if hl == 0 else []),
                        sig="TC" if hl == 3 else None)
                ps3 = self.P4[:, 0:260].rearrange("p (hh c) -> p hh c", hh=4)[:, :, 0:65]
                e_add = p.step("vector", lambda v, srun3=srun3, ps3=ps3: v.tensor_tensor(srun3, srun3, ps3, ALU.add),
                               after=[e_mm, e_snap], sig="VC")
                e_sn.append(e_snap)
            e_state_done = e_add
            for _ in qgen:
                pass
            pq_e, pq_q = self.proj_res[kbase(g)]
            e_qM = []
            for hl in range(4):
                _, e_M = self.phi(4 * g + hl, None, PH + 8192 + 2048 * hl, rot, True,
                                  [pq_e, pq_q, e_state_done], kbase(g))
                e_qM.append(e_M)
                rot += 1
            kgen = None
            if g < 3:
                e_wkn = self.load_w_slice(self.wkp, g + 1, 256, WS + 2048 * ((g + 1) % 2), (g + 1) % 2)
                kgen = self.proj_pair_gen(e_wkn, WS + 2048 * ((g + 1) % 2), (g + 1) % 2, x_tl, SCALE, kbase(g + 1))
            amt = [XS, XS + 512]
            atn = [XS + 1024, XS + 1280]
            prev_eam = None
            prev_amt_user = [None, None]
            prev_atn_user = [None, None]
            prev_numuser = None
            prev_den = None
            for j in range(NCH):
                par = j % 2
                if kgen is not None and j % 2 == 1:
                    next(kgen, None)
                e_at = None
                for hl in range(4):
                    e_at = p.step("tensor", lambda t, hl=hl, j=j: t.matmul(
                        self.P5[:, 128 * hl:128 * (hl + 1)],
                        self.bf(PH + 2048 * hl + 128 * j, 128),
                        self.bf(PH + 8192 + 2048 * hl + 128 * j, 128),
                        start=True, stop=True),
                        after=([e_kM[hl], e_qM[hl]] if j < 2 else []) + ([prev_eam] if hl == 0 else []),
                        sig="TC" if hl == 3 else None)
                ps3 = self.P5[:, 0:512].rearrange("p (hh c) -> p hh c", hh=4)
                mb = self.C[:, 0:128][:, None, :].broadcast_to([128, 4, 128])
                am3 = self.bf(amt[par], 512).rearrange("p (hh c) -> p hh c", hh=4)
                e_am = p.step("vector", lambda v, am3=am3, ps3=ps3, mb=mb: v.tensor_tensor(am3, ps3, mb, ALU.mult),
                              after=[e_at, self.e_utmask, prev_amt_user[par]], sig="VC")
                prev_eam = e_am
                e_nm = None
                for hl in range(4):
                    p.step("tensor", lambda t, hl=hl, j=j, g=g, par=par: t.matmul(
                        self.P6[:, 65 * hl:65 * hl + 65],
                        self.bf(amt[par] + 128 * hl, 128),
                        self.A[:, VT + 1056 * j + 66 * (4 * g + hl): VT + 1056 * j + 66 * (4 * g + hl) + 65],
                        start=True, stop=False),
                        after=[e_am] + ([prev_numuser] if hl == 0 else []), sig=None)
                    e_nm = p.step("tensor", lambda t, hl=hl, j=j, g=g: t.matmul(
                        self.P6[:, 65 * hl:65 * hl + 65],
                        self.bf(PH + 8192 + 2048 * hl + 128 * j, 128),
                        self.A[:, SN + 1056 * (4 * g + hl) + 66 * j: SN + 1056 * (4 * g + hl) + 66 * j + 65],
                        start=False, stop=True),
                        after=[e_sn[j]] if hl == 0 else [], sig="TC" if hl == 3 else None)
                prev_amt_user[par] = e_nm
                den3 = self.P6[:, 0:260].rearrange("p (hh c) -> p hh c", hh=4)[:, :, 64:65]
                if g == 1 and j == 0 and "ps6" in TAPS:
                    t14 = self.nc.dram_tensor("dbg_amtl", [128, 512], BF16, kind="ExternalOutput").ap()
                    p.step("sync", lambda s, t14=t14, par=par: s.dma_start(out=t14, in_=self.bf(amt[par], 512)),
                           after=[e_nm], sig="SQ", dma=True)
                    t15 = self.nc.dram_tensor("dbg_snapl", [128, 260], BF16, kind="ExternalOutput").ap()
                    p.step("sync", lambda s, t15=t15, g=g, j=j: s.dma_start(
                        out=t15, in_=self.A[:, SN + 4224 * g + 66 * j: SN + 4224 * g + 66 * j + 260]),
                        after=[e_nm], sig="SQ", dma=True)
                    t16b = self.nc.dram_tensor("dbg_qpml", [128, 128], BF16, kind="ExternalOutput").ap()
                    p.step("sync", lambda s, t16b=t16b, j=j: s.dma_start(out=t16b, in_=self.bf(PH + 8192 + 128 * j, 128)),
                           after=[e_nm], sig="SQ", dma=True)
                    t12 = self.nc.dram_tensor("dbg_ps6", [128, 260], F32, kind="ExternalOutput").ap()
                    e_cp6 = p.step("vector", lambda v: v.tensor_copy(self.f32(XS + 4096, 260), self.P6[:, 0:260]),
                                   after=[e_nm], sig="VC")
                    p.step("sync", lambda s, t12=t12: s.dma_start(out=t12, in_=self.f32(XS + 4096, 260)),
                           after=[e_cp6], sig="SQ", dma=True)
                e_d1 = p.step("vector", lambda v, den3=den3: v.tensor_scalar(self.CF[:, 424:428], den3, FEPS, None, ALU.add),
                              after=[e_nm, prev_den], sig="VC")
                e_d2 = p.step("vector", lambda v: v.reciprocal(self.CF[:, 428:432], self.CF[:, 424:428]), after=[e_d1], sig="VC")
                num3 = self.P6[:, 0:260].rearrange("p (hh c) -> p hh c", hh=4)[:, :, 0:64]
                rb3 = self.CF[:, 428:432][:, :, None].broadcast_to([128, 4, 64])
                at3 = self.bf(atn[par], 256).rearrange("p (hh c) -> p hh c", hh=4)
                e_at2 = p.step("vector", lambda v, at3=at3, num3=num3, rb3=rb3: v.tensor_tensor(at3, num3, rb3, ALU.mult),
                               after=[e_d2, prev_atn_user[par]], sig="VC")
                if g == 1 and j == 0 and "ps6" in TAPS:
                    t13 = self.nc.dram_tensor("dbg_rden", [128, 16], F32, kind="ExternalOutput").ap()
                    p.step("sync", lambda s, t13=t13: s.dma_start(out=t13, in_=self.CF[:, 424:440]),
                           after=[e_at2], sig="SQ", dma=True)
                prev_numuser = e_at2
                prev_den = e_at2
                e_flush = p.step("vector", lambda v: v.memset(self.CF[:, 439:440], 0.0),
                                 after=[e_at2], sig="VC")
                ev = p.step("scalar", lambda s, j=j, g=g, par=par: s.dma_start(
                    out=self.scrA[128 * j:128 * (j + 1), 256 * g:256 * (g + 1)], in_=self.bf(atn[par], 256)),
                    after=[e_flush], sig="AQ", dma=True)
                if g == 1 and j == 0 and "sc10" in TAPS:
                    t10 = self.nc.dram_tensor("dbg_sc10", [128, 256], BF16, kind="ExternalOutput").ap()
                    t11 = self.nc.dram_tensor("dbg_sb10", [128, 256], BF16, kind="ExternalOutput").ap()
                    p.step("sync", lambda s, t10=t10: s.dma_start(out=t10, in_=self.scrA[0:128, 256:512]),
                           after=[ev], sig="SQ", dma=True)
                    p.step("sync", lambda s, t11=t11, par=par: s.dma_start(out=t11, in_=self.bf(atn[par], 256)),
                           after=[ev], sig="SQ", dma=True)
                prev_atn_user[par] = ev
                e_attn_last = ev
            if kgen is not None:
                for _ in kgen:
                    pass
            p.barrier()

        e_afm = []
        for c in range(8):
            e_afm.append(p.step("sync", lambda s, c=c: s.dma_start(
                out=self.bf(QD + 2048 * c, 2048), in_=self.scrA[:, 128 * c:128 * (c + 1)], transpose=True),
                after=[e_attn_last], sig="SQ", dma=True))
        if "scra" in TAPS:
            t = self.nc.dram_tensor("dbg_scra", [T, D], BF16, kind="ExternalOutput").ap()
            self.p.step("sync", lambda s, t=t: s.dma_start(out=t, in_=self.scrA), after=[e_attn_last], sig="SQ", dma=True)
            self.p.barrier()
        self.tap("attnfm", self.bf(QD, 16384), [e_afm[7]])
        e_y = {}
        for c in range(8):
            slot = c % 2
            e_w = self.load_w_slice(self.wop, c, 128, WS + 2048 * slot, slot)
            last_mm = None
            for tt in range(4):
                bank = self.PBG[:, 512 * tt:512 * (tt + 1)]
                e_mm = None
                for kc in range(8):
                    e_mm = p.step("tensor", lambda t, c=c, tt=tt, kc=kc, bank=bank, slot=slot: t.matmul(
                        bank, self.bf(WS + 2048 * slot + 128 * kc, 128),
                        self.bf(QD + 2048 * kc + 512 * tt, 512),
                        start=(kc == 0), stop=(kc == 7)),
                        after=[e_w, e_afm[kc]] + ([e_y.get((c - 1, tt))] if c >= 1 else []),
                        sig="TC" if kc == 7 else None)
                e_y[(c, tt)] = p.step("vector", lambda v, c=c, tt=tt, bank=bank: v.tensor_tensor(
                    self.bf(PH + 2048 * c + 512 * tt, 512), bank,
                    self.bf(XN + 2048 * c + 512 * tt, 512), ALU.add),
                    after=[e_mm], sig="VC")
                last_mm = e_mm
            self.ws_user[slot] = last_mm
        self.tap("y", self.bf(PH, 16384), [e_y[(7, 3)]])
        p.barrier()

        muR = self.f32(WS, 2048)[0:1, :]
        s2R = self.f32(XS + 6144, 2048)[0:1, :]
        e_rowmu = []
        e_rows2 = []
        prev_sqv = None
        for tt in range(4):
            e_m1 = None
            e_m2 = None
            for c in range(8):
                e_q = p.step("scalar", lambda s, c=c, tt=tt: s.activation(
                    self.bf(XS + 4096, 512), self.bf(PH + 2048 * c + 512 * tt, 512), AF.Square),
                    after=[e_y[(c, tt)], prev_sqv], sig="SC")
                e_m1 = p.step("tensor", lambda t, c=c, tt=tt: t.matmul(
                    self.P7[0:1, 0:512], self.C[:, 128:129], self.bf(PH + 2048 * c + 512 * tt, 512),
                    start=(c == 0), stop=(c == 7)),
                    after=[e_y[(c, tt)]] + ([e_rowmu[-1]] if (c == 0 and e_rowmu) else []),
                    sig="TC" if c == 7 else None)
                e_m2 = p.step("tensor", lambda t, c=c, tt=tt: t.matmul(
                    self.P6[0:1, 0:512], self.C[:, 128:129], self.bf(XS + 4096, 512),
                    start=(c == 0), stop=(c == 7)),
                    after=[e_q] + ([e_rows2[-1]] if (c == 0 and e_rows2) else []),
                    sig="TC" if c == 7 else None)
                prev_sqv = e_m2
            e_rowmu.append(p.step("vector", lambda v, tt=tt: v.tensor_copy(muR[0:1, 512 * tt:512 * (tt + 1)], self.P7[0:1, 0:512]),
                                  after=[e_m1], sig="VC"))
            e_rows2.append(p.step("vector", lambda v, tt=tt: v.tensor_copy(s2R[0:1, 512 * tt:512 * (tt + 1)], self.P6[0:1, 0:512]),
                                  after=[e_m2], sig="VC"))
        vR = s2R
        e_nm = p.step("vector", lambda v: v.tensor_scalar(muR, muR, -1.0 / 1024.0, None, ALU.mult),
                      after=[e_rowmu[-1]], sig="VC")
        mu2R = self.f32(XS, 2048)[0:1, :]
        e_mu2 = p.step("vector", lambda v: v.tensor_tensor(mu2R, muR, muR, ALU.mult), after=[e_nm], sig="VC")
        e_v = p.step("vector", lambda v: v.scalar_tensor_tensor(vR, s2R, 1.0 / 1024.0, mu2R, ALU.mult, ALU.subtract),
                     after=[e_mu2, e_rows2[-1]], sig="VC")
        e_l = p.step("scalar", lambda s: s.activation(vR, vR, AF.Ln, bias=self.CF[0:1, 147:148]), after=[e_v], sig="SC")
        e_r = p.step("scalar", lambda s: s.activation(self.bf(PH + 16384, 2048)[0:1, :], vR, AF.Exp, scale=-0.5),
                     after=[e_l], sig="SC")
        e_br = p.step("vector", lambda v: v.tensor_tensor(
            self.bf(PH + 18432, 2048)[0:1, :], muR, self.bf(PH + 16384, 2048)[0:1, :], ALU.mult),
            after=[e_r, e_nm], sig="VC")
        e_bc = []
        for (src0, dst0) in ((PH + 16384, XS), (PH + 18432, XS + 2048)):
            for tt in range(4):
                e_mm = p.step("tensor", lambda t, src0=src0, tt=tt: t.matmul(
                    self.P7[:, 0:512], self.C[0:1, 129:257], self.bf(src0, 2048)[0:1, 512 * tt:512 * (tt + 1)],
                    start=True, stop=True),
                    after=[e_r if src0 == PH + 16384 else e_br, self.e_ones] + ([e_bc[-1]] if e_bc else []),
                    sig="TC")
                e_bc.append(p.step("vector", lambda v, dst0=dst0, tt=tt: v.tensor_copy(
                    self.bf(dst0 + 512 * tt, 512), self.P7[:, 0:512]), after=[e_mm], sig="VC"))
        e_bcl = e_bc[-1]
        e_h2 = []
        for c in range(8):
            e_1 = p.step("vector", lambda v, c=c: v.tensor_tensor(
                self.bf(H2 + 2048 * c, 2048), self.bf(PH + 2048 * c, 2048), self.bf(XS, 2048), ALU.mult),
                after=[e_bcl, e_y[(c, 3)]] + ([e_h2[-1]] if e_h2 else []), sig="VC")
            e_2 = p.step("vector", lambda v, c=c: v.tensor_tensor(
                self.bf(H2 + 2048 * c, 2048), self.bf(H2 + 2048 * c, 2048), self.bf(XS + 2048, 2048), ALU.add),
                after=[e_1], sig="VC")
            e_h2.append(p.step("scalar", lambda s, c=c: s.activation(
                self.bf(H2 + 2048 * c, 2048), self.bf(H2 + 2048 * c, 2048), AF.Identity,
                bias=self.CF[:, 8 + c:9 + c], scale=self.CF[:, 0 + c:1 + c]),
                after=[e_2], sig="SC"))
        self.tap("h2", self.bf(H2, 16384), [e_h2[-1]])
        p.barrier()

        tmp = XS
        prev_c = None
        e_mix = []
        for (mu0, om0, dst0) in ((16, 32, PH), (24, 40, QD)):
            for c in range(8):
                e_a = p.step("vector", lambda v, c=c, om0=om0: v.tensor_scalar(
                    self.bf(tmp + 1, 2047), self.bf(XN + 2048 * c, 2047), self.CF[:, om0 + c:om0 + c + 1], None, ALU.mult),
                    after=[prev_c], sig="VC")
                e_b = p.step("vector", lambda v, c=c, om0=om0: v.tensor_scalar(
                    self.bf(tmp, 1), self.C2[:, 2096 + c:2097 + c], self.CF[:, om0 + c:om0 + c + 1], None, ALU.mult),
                    after=[e_a], sig="VC")
                e_c = p.step("vector", lambda v, c=c, mu0=mu0, dst0=dst0: v.scalar_tensor_tensor(
                    self.bf(dst0 + 2048 * c, 2048), self.bf(XN + 2048 * c, 2048),
                    self.CF[:, mu0 + c:mu0 + c + 1], self.bf(tmp, 2048), ALU.mult, ALU.add),
                    after=[e_b], sig="VC")
                prev_c = e_c
                e_mix.append(e_c)
        self.tap("xr", self.bf(PH, 16384), [e_mix[-1]])
        self.tap("xk", self.bf(QD, 16384), [e_mix[-1]])
        p.barrier()
        for (wsrc, xsrc, kind) in ((self.wk2p, QD, "k"), (self.wrtp, PH, "r")):
            for c in range(8):
                slot = c % 2
                e_w = self.load_w_slice(wsrc, c, 128, WS + 2048 * slot, slot)
                last_mm = None
                for tt in range(4):
                    bank = self.PBG[:, 512 * tt:512 * (tt + 1)]
                    e_mm = None
                    for kc in range(8):
                        e_mm = p.step("tensor", lambda t, c=c, tt=tt, kc=kc, bank=bank, slot=slot, xsrc=xsrc: t.matmul(
                            bank, self.bf(WS + 2048 * slot + 128 * kc, 128),
                            self.bf(xsrc + 2048 * kc + 512 * tt, 512),
                            start=(kc == 0), stop=(kc == 7)),
                            after=[e_w, e_mix[-1]] + ([self.kk_bank_user.get((c - 1, tt))] if c >= 1 else []),
                            sig="TC" if kc == 7 else None) if True else None
                    if kind == "k":
                        e_rl = p.step("vector", lambda v, c=c, tt=tt, bank=bank: v.tensor_scalar(
                            self.bf(XS + 4096, 512), bank, 0.0, None, ALU.max),
                            after=[e_mm, self.kk_tmp_user], sig="VC")
                        e_kq = p.step("vector", lambda v, c=c, tt=tt: v.tensor_tensor(
                            self.bf(XN + 2048 * c + 512 * tt, 512), self.bf(XS + 4096, 512), self.bf(XS + 4096, 512), ALU.mult),
                            after=[e_rl], sig="VC")
                        self.kk_tmp_user = e_kq
                        self.kk_bank_user[(c, tt)] = e_kq
                    else:
                        e_sg = p.step("scalar", lambda s, c=c, tt=tt, bank=bank: s.activation(
                            self.bf(SN + 2048 * c + 512 * tt, 512), bank, AF.Sigmoid),
                            after=[e_mm], sig="SC")
                        self.kk_bank_user[(c, tt)] = e_sg
                        ev = p.step("scalar", lambda s, c=c, tt=tt: s.dma_start(
                            out=self.rscr[128 * c:128 * (c + 1), 512 * tt:512 * (tt + 1)],
                            in_=self.bf(SN + 2048 * c + 512 * tt, 512)),
                            after=[e_sg], sig="AQ", dma=True)
                        self.r_store = ev
                    last_mm = e_mm
                self.ws_user[slot] = last_mm
            p.barrier()
        self.tap("kact", self.bf(XN, 16384), [self.kk_tmp_user])
        e_rt = []
        for tc in range(NCH):
            e_rt.append(p.step("sync", lambda s, tc=tc: s.dma_start(
                out=self.bf(SN + 1024 * tc, 1024), in_=self.rscr[:, 128 * tc:128 * (tc + 1)], transpose=True),
                after=[self.r_store], sig="SQ", dma=True))
        self.tap("rtok", self.bf(SN, 16384), [e_rt[15]])
        e_wv2t = p.step("gpsimd", lambda g: g.dma_start(
            out=self.bf(PH, 8192).rearrange("p (kc c) -> p kc c", kc=8),
            in_=self.wv2t.rearrange("(kc p) c -> p kc c", p=128)), sig="LD", dma=True)
        p.barrier()

        w2out = self.f32(XS + 4096, 1024)
        a1 = self.f32(XS + 4096, 1024)
        e_out_final = None
        prev_gbank = [None, None]
        prev_w2mm = {}
        prev_asm = {}
        w1slot = lambda d: WS + 1024 * (d % 4)
        w2slot = lambda d: PH + 8192 + 1024 * (d % 4)
        w1_user = [None] * 4
        w2_user = [None] * 4
        e_wst = self.e_wstage
        for tt in range(4):
            e_gev = {}
            for dff in range(32):
                par = dff % 2
                e_w1 = p.step("scalar", lambda s, dff=dff: s.dma_start(
                    out=self.bf(w1slot(dff), 1024), in_=self.wb1[128 * dff:128 * (dff + 1), :]),
                    after=[w1_user[dff % 4], e_wst[0]], sig="AQ", dma=True)
                bank = self.PBG[:, 1024 * par:1024 * par + 512]
                e_mm = None
                for kc in range(8):
                    e_mm = p.step("tensor", lambda t, dff=dff, kc=kc, tt=tt, bank=bank: t.matmul(
                        bank, self.bf(w1slot(dff) + 128 * kc, 128),
                        self.bf(H2 + 2048 * kc + 512 * tt, 512),
                        start=(kc == 0), stop=(kc == 7)),
                        after=[e_w1, e_h2[kc], prev_gbank[par]] + ([prev_asm.get(tt - 1)] if kc == 0 else []),
                        sig="TC" if kc == 7 else None)
                w1_user[dff % 4] = e_mm
                e_g = p.step("vector", lambda v, dff=dff, bank=bank: v.tensor_scalar(
                    self.bf(QD + 512 * dff, 512), bank, self.CF[:, 48 + dff:49 + dff], 0.0, ALU.add, ALU.max),
                    after=[e_mm], sig="VC")
                prev_gbank[par] = e_g
                e_gev[dff] = e_g
            banks = [self.PBG[:, 512 * i:512 * (i + 1)] for i in range(4)] + [self.P4, self.P5, self.P6, self.P7]
            e_w2mm = {}
            for dff in range(32):
                e_w2 = p.step("sync", lambda s, dff=dff: s.dma_start(
                    out=self.bf(w2slot(dff), 1024), in_=self.wb2[128 * dff:128 * (dff + 1), :]),
                    after=[w2_user[dff % 4], e_wst[1]], sig="SQ", dma=True)
                for tl in range(4):
                    for dh in range(2):
                        bk = banks[tl + 4 * dh]
                        e_w2mm[(tl, dh)] = p.step("tensor", lambda t, dff=dff, tl=tl, dh=dh, bk=bk: t.matmul(
                            bk[:, 0:512],
                            self.bf(QD + 512 * dff + 128 * tl, 128),
                            self.bf(w2slot(dff) + 512 * dh, 512),
                            start=(dff == 0), stop=(dff == 31)),
                            after=[e_w2, e_gev[dff]] + ([prev_w2mm.get((tl, dh)), e_gev[31]] if dff == 0 else []),
                            sig="TC" if (tl == 3 and dh == 1) else None)
                w2_user[dff % 4] = e_w2mm[(3, 1)]
            if tt == 0 and "gsb2" in TAPS:
                tg1 = self.nc.dram_tensor("dbg_gsb2", [128, 16384], BF16, kind="ExternalOutput").ap()
                p.step("sync", lambda s, tg1=tg1: s.dma_start(out=tg1, in_=self.bf(QD, 16384)),
                       after=[e_w2mm[(3, 1)]], sig="SQ", dma=True)
                tg2 = self.nc.dram_tensor("dbg_w2sl", [128, 1024], BF16, kind="ExternalOutput").ap()
                p.step("sync", lambda s, tg2=tg2: s.dma_start(out=tg2, in_=self.bf(WS + 4096, 1024)),
                       after=[e_w2mm[(3, 1)]], sig="SQ", dma=True)
                e_bc0 = p.step("vector", lambda v: v.tensor_copy(self.f32(XS + 8192, 512), banks[0][:, 0:512]),
                               after=[e_w2mm[(3, 1)]], sig="VC")
                tg3 = self.nc.dram_tensor("dbg_bank0", [128, 512], F32, kind="ExternalOutput").ap()
                p.step("sync", lambda s, tg3=tg3: s.dma_start(out=tg3, in_=self.f32(XS + 8192, 512)),
                       after=[e_bc0], sig="SQ", dma=True)
            if tt == 0:
                self.tap("gsb", self.bf(QD, 16384), [e_gev[31]])
            for tl in range(4):
                tc = 4 * tt + tl
                bk0 = banks[tl]
                bk1 = banks[tl + 4]
                e_e0 = p.step("vector", lambda v, bk0=bk0: v.tensor_tensor(
                    w2out[:, 0:512], bk0[:, 0:512], self.C2[:, 1056:1568], ALU.add),
                    after=[e_w2mm[(3, 1)], prev_asm.get(tt)], sig="VC")
                e_e1 = p.step("vector", lambda v, bk1=bk1: v.tensor_tensor(
                    w2out[:, 512:1024], bk1[:, 0:512], self.C2[:, 1568:2080], ALU.add),
                    after=[e_e0], sig="VC")
                if tt == 0 and tl == 0:
                    self.tap("w2o", w2out, [e_e1])
                e_v0 = None
                e_v1 = None
                for dh in range(2):
                    bk = bk0 if dh == 0 else bk1
                    e_mmv = None
                    for kc in range(8):
                        e_mmv = p.step("tensor", lambda t, tc=tc, dh=dh, kc=kc, bk=bk: t.matmul(
                            bk[:, 0:512],
                            self.bf(XN + 2048 * kc + 128 * tc, 128),
                            self.bf(PH + 1024 * kc + 512 * dh, 512),
                            start=(kc == 0), stop=(kc == 7)),
                            after=[e_wv2t, e_e1 if dh == 0 else None, e_e1 if dh == 1 else None],
                            sig="TC" if kc == 7 else None)
                    if dh == 0:
                        e_v0 = e_mmv
                    else:
                        e_v1 = e_mmv
                xio = self.f32(XS + 6144 + 2048 * (tl % 2), 1024)
                e_x = p.step("sync", lambda s, tc=tc, xio=xio: s.dma_start(
                    out=xio, in_=self.x_tok[128 * tc:128 * (tc + 1), :]),
                    after=[prev_asm.get((tt, tl % 2))], sig="SQ", dma=True)
                e_p1 = p.step("gpsimd", lambda g, xio=xio: g.tensor_tensor(xio, xio, w2out, ALU.add),
                              after=[e_x, e_e1], sig="PC")
                e_a0 = p.step("vector", lambda v, bk0=bk0, tc=tc: v.tensor_tensor(
                    a1[:, 0:512], bk0[:, 0:512], self.bf(SN + 1024 * tc, 512), ALU.mult),
                    after=[e_v0, e_rt[tc], e_e1, e_p1], sig="VC")
                e_a1 = p.step("vector", lambda v, bk1=bk1, tc=tc: v.tensor_tensor(
                    a1[:, 512:1024], bk1[:, 0:512], self.bf(SN + 1024 * tc + 512, 512), ALU.mult),
                    after=[e_v1, e_a0], sig="VC")
                if tt == 0 and tl == 0:
                    self.tap("a1t", a1, [e_a1])
                e_p2 = p.step("vector", lambda v, xio=xio: v.tensor_tensor(xio, xio, a1, ALU.add),
                              after=[e_p1, e_a1], sig="VC")
                e_st = p.step("sync", lambda s, tc=tc, xio=xio: s.dma_start(
                    out=self.out[128 * tc:128 * (tc + 1), :], in_=xio),
                    after=[e_p2], sig="SQ", dma=True)
                prev_asm[(tt, tl % 2)] = e_st
                prev_asm[tt] = e_a1
                prev_w2mm[(tl, 0)] = e_a0
                prev_w2mm[(tl, 1)] = e_a1
                e_out_final = e_st
            p.barrier()
        p.step("sync", lambda s: s.nop(), after=[e_out_final])
        return self


def build_program():
    b = B()
    b.kk_bank_user = {}
    b.kk_tmp_user = None
    b.r_store = None
    b.build()
    nc = b.nc
    sems = {}
    import contextlib
    stack = contextlib.ExitStack()
    for name in b.p.count:
        sems[name] = stack.enter_context(nc.semaphore(name))
    with nc.Block() as block:
        block.tensor(lambda t: b.p.emit_engine("tensor", t, sems))
        block.vector(lambda v: b.p.emit_engine("vector", v, sems))
        block.scalar(lambda s: b.p.emit_engine("scalar", s, sems))
        block.gpsimd(lambda g: b.p.emit_engine("gpsimd", g, sems))
        block.sync(lambda s: b.p.emit_engine("sync", s, sems))
    stack.close()
    return nc


_NC_CACHE = None
_LAST_RES = None


def kernel(**inputs):
    global _NC_CACHE
    x = np.asarray(inputs["L0_attn_batch"])
    wq = np.asarray(inputs["L0_attn_Wq"])
    wk = np.asarray(inputs["L0_attn_Wk"])
    wv = np.asarray(inputs["L0_attn_Wv"])
    wo = np.asarray(inputs["L0_attn_Wo"])
    omega = np.asarray(inputs["L0_attn_omega"])
    w1 = np.asarray(inputs["L0_attn_W1"])
    b1 = np.asarray(inputs["L0_attn_b1"])
    w2 = np.asarray(inputs["L0_attn_W2"])
    b2 = np.asarray(inputs["L0_attn_b2"])
    gamma = np.asarray(inputs["L0_attn_gamma"])
    beta = np.asarray(inputs["L0_attn_beta"])
    wr = np.asarray(inputs["L0_ffn_W_r"])
    wk2 = np.asarray(inputs["L0_ffn_W_k"])
    wv2 = np.asarray(inputs["L0_ffn_W_v"])
    mur = np.asarray(inputs["L0_ffn_mu_r"])
    muk = np.asarray(inputs["L0_ffn_mu_k"])

    def pack_lhst(w, ncols):
        nsl = w.shape[1] // ncols
        a = w.reshape(8, 128, nsl, ncols)
        a = np.transpose(a, (2, 1, 0, 3))
        return np.ascontiguousarray(a.reshape(nsl * 128, 8 * ncols))

    com = {
        "wqp": pack_lhst(wq, 256), "wkp": pack_lhst(wk, 256), "wv": wv,
        "wop": pack_lhst(wo, 128),
        "womega_t": np.ascontiguousarray(omega.T),
        "w1p": pack_lhst(w1, 128), "b1_t": np.ascontiguousarray(b1.reshape(32, 128).T),
        "w2": w2, "b2row": b2.reshape(1, 1024),
        "gamma_t": np.ascontiguousarray(gamma.reshape(8, 128).T),
        "beta_t": np.ascontiguousarray(beta.reshape(8, 128).T),
        "wrtp": pack_lhst(np.ascontiguousarray(wr.T), 128),
        "wk2p": pack_lhst(np.ascontiguousarray(wk2.T), 128),
        "wv2t": np.ascontiguousarray(wv2.T),
        "mur_t": np.ascontiguousarray(mur.reshape(8, 128).T),
        "muk_t": np.ascontiguousarray(muk.reshape(8, 128).T),
        "utmask": np.triu(np.ones((128, 128), np.float32)),
    }
    com = {k: np.ascontiguousarray(v, dtype=np.float32) for k, v in com.items()}
    zrow = np.zeros((128, 8), np.float32)
    zpre = np.zeros((2048, 1024), np.float32)
    in_maps = []
    for core in range(8):
        bb, half = core // 2, core % 2
        m = dict(com)
        m["x_tok"] = np.ascontiguousarray(x[bb, 2048 * half:2048 * (half + 1)])
        m["xp_tok"] = np.ascontiguousarray(x[bb, 0:2048]) if half == 1 else zpre
        m["prevrow_fm"] = (np.ascontiguousarray(x[bb, 2047].reshape(8, 128).T)
                           if half == 1 else zrow)
        m["smask"] = np.full((128, 1), float(half), np.float32)
        in_maps.append(m)

    if _NC_CACHE is None:
        _NC_CACHE = build_program()
    res = run_bass_kernel_spmd(_NC_CACHE, in_maps, core_ids=list(range(8)))
    global _LAST_RES
    _LAST_RES = res
    out = np.empty((4, 4096, 1024), np.float32)
    for core in range(8):
        bb, half = core // 2, core % 2
        out[bb, 2048 * half:2048 * (half + 1)] = res.results[core]["out"]
    return out


# version 09 (final)
# speedup vs baseline: 1.3460x; 1.0501x over previous
import sys
if '/opt/trn_rl_repo' not in sys.path:
    sys.path.insert(0, '/opt/trn_rl_repo')
import math
import numpy as np
import ml_dtypes
import concourse.bass as bass
import concourse.mybir as mybir
from concourse.bass_utils import run_bass_kernel_spmd

F32 = mybir.dt.float32
BF16 = mybir.dt.bfloat16
AF = mybir.ActivationFunctionType
ALU = mybir.AluOpType
AX = mybir.AxisListType

T = 2048
NCH = 16
D = 1024
SCALE = 64.0 ** -0.25
FEPS = 1e-6
SQM = float(math.sqrt(128.0))

import os
KTRUNC = int(os.environ.get("KTRUNC", "99"))
ENGINES = ("tensor", "vector", "scalar", "gpsimd", "sync")
TAPS = set()
TAPROT = -1

XN = 0
VT = 16384
SN = 33280
QD = 50176
PH = 66560
WS = 87040
XS = 92160
ACOLS = 102400

H2 = VT


class Prog:
    def __init__(self):
        self.steps = {e: [] for e in ENGINES}
        self.count = {s: 0 for s in ("LD", "SQ", "AQ", "TC", "VC", "SC", "PC")}
        self.pending = {e: () for e in ENGINES}

    def step(self, eng, fn, after=(), sig=None, dma=False):
        after = tuple(e for e in after if e is not None) + self.pending[eng]
        self.pending[eng] = ()
        incs = ()
        ev = None
        if sig is not None:
            n = 16 if dma else 1
            self.count[sig] += n
            ev = (sig, self.count[sig])
            incs = ((sig, n),)
        self.steps[eng].append((fn, after, incs))
        return ev

    def barrier(self):
        evs = tuple((s, v) for s, v in self.count.items() if v > 0)
        for e in ENGINES:
            self.pending[e] = self.pending[e] + evs

    def emit_engine(self, eng_name, eng, sems):
        seen = {}
        for fn, waits, incs in self.steps[eng_name]:
            for (s, v) in waits:
                if seen.get(s, 0) < v:
                    eng.wait_ge(sems[s], v)
                    seen[s] = v
            inst = fn(eng)
            for (s, n) in incs:
                inst.then_inc(sems[s], n)


class B:
    def __init__(self):
        nc = bass.Bass("TRN2", target_bir_lowering=False)
        self.nc = nc
        self.p = Prog()

        def inp(name, shape):
            return nc.dram_tensor(name, list(shape), F32, kind="ExternalInput").ap()

        self.x_tok = inp("x_tok", (T, D))
        self.xp_tok = inp("xp_tok", (T, D))
        self.prevrow = inp("prevrow_fm", (128, 8))
        self.smask = inp("smask", (128, 1))
        self.wqp = inp("wqp", (512, 2048))
        self.wkp = inp("wkp", (512, 2048))
        self.wv = inp("wv", (D, D))
        self.wop = inp("wop", (1024, 1024))
        self.womega_t = inp("womega_t", (64, 128))
        self.w1p = inp("w1p", (4096, 1024))
        self.b1_t = inp("b1_t", (128, 32))
        self.w2 = inp("w2", (4096, D))
        self.b2row = inp("b2row", (1, D))
        self.gamma_t = inp("gamma_t", (128, 8))
        self.beta_t = inp("beta_t", (128, 8))
        self.wrtp = inp("wrtp", (1024, 1024))
        self.wk2p = inp("wk2p", (1024, 1024))
        self.wv2t = inp("wv2t", (D, D))
        self.mur_t = inp("mur_t", (128, 8))
        self.muk_t = inp("muk_t", (128, 8))
        self.utmask_in = inp("utmask", (128, 128))

        self.out = nc.dram_tensor("out", [T, D], F32, kind="ExternalOutput").ap()

        self.scrA = nc.dram_tensor("scrA", [T, D], BF16, kind="Internal").ap()
        self.wb1 = nc.dram_tensor("wb1", [4096, 1024], BF16, kind="Internal").ap()
        self.wb2 = nc.dram_tensor("wb2", [4096, 1024], BF16, kind="Internal").ap()
        self.rscr = nc.dram_tensor("rscr", [D, T], BF16, kind="Internal").ap()

        self.arena = nc.alloc_sbuf_tensor("arena", [128, ACOLS], BF16)
        self.A = self.arena.ap()
        self.cst = nc.alloc_sbuf_tensor("cst", [128, 772], BF16)
        self.C = self.cst.ap()
        self.cst2 = nc.alloc_sbuf_tensor("cst2", [128, 2120], BF16)
        self.C2 = self.cst2.ap()
        self.cstf = nc.alloc_sbuf_tensor("cstf", [128, 480], F32)
        self.CF = self.cstf.ap()

        self.pbig = nc.alloc_psum_tensor("pbig", [128, 2048], F32)
        self.PBG = self.pbig.ap()
        self.P4 = nc.alloc_psum_tensor("ps4", [128, 512], F32).ap()
        self.P5 = nc.alloc_psum_tensor("ps5", [128, 512], F32).ap()
        self.P6 = nc.alloc_psum_tensor("ps6", [128, 512], F32).ap()
        self.P7 = nc.alloc_psum_tensor("ps7", [128, 512], F32).ap()
        self.PT = self.P6.bitcast(BF16)

        self.ws_user = [None, None, None]
        self.pbg_user = [None, None, None, None]
        self.z_sub = [None, None]
        self.pt_chain = None
        self.scr_T = [None, None]
        self.zc_exp = [None, None]
        self.kk_bank_user = {}
        self.kk_tmp_user = None
        self.r_store = None

    def tap(self, tag, ap, after):
        if tag not in TAPS:
            return
        t = self.nc.dram_tensor("dbg_" + tag, list(ap.shape), ap.dtype, kind="ExternalOutput").ap()
        self.p.step("sync", lambda s, t=t, ap=ap: s.dma_start(out=t, in_=ap), after=after, sig="SQ", dma=True)
        self.p.barrier()

    def bf(self, c0, n):
        return self.A[:, c0:c0 + n]

    def f32(self, c0, n):
        assert c0 % 2 == 0
        return self.A.bitcast(F32)[:, c0 // 2: c0 // 2 + n]

    def consts(self):
        p, C, C2, CF = self.p, self.C, self.C2, self.CF
        p.step("gpsimd", lambda g: g.dma_start(out=C[:, 0:128], in_=self.utmask_in), sig="LD", dma=True)
        e_z0 = p.step("vector", lambda v: v.memset(C[:, 260:388], 0.0), sig="VC")
        e_z1 = p.step("vector", lambda v: v.memset(C[:, 644:772], 0.0), sig="VC")
        p.step("gpsimd", lambda g: g.dma_start(out=C[0:64, 260:388], in_=self.womega_t),
               after=[e_z0], sig="LD", dma=True)
        p.step("gpsimd", lambda g: g.dma_start(out=C[64:128, 644:772], in_=self.womega_t),
               after=[e_z1], sig="LD", dma=True)
        p.step("gpsimd", lambda g: g.dma_start(out=self.bf(XS, 1024)[0:1, :], in_=self.b2row), sig="LD", dma=True)
        for (ap, c0) in ((self.gamma_t, 0), (self.beta_t, 8), (self.mur_t, 16), (self.muk_t, 24),
                         (self.b1_t, 48), (self.smask, 80), (self.prevrow, 136)):
            n = ap.shape[1]
            p.step("gpsimd", lambda g, ap=ap, c0=c0, n=n: g.dma_start(out=CF[:, c0:c0 + n], in_=ap), sig="LD", dma=True)
        e_cd = ("LD", p.count["LD"])

        p.step("vector", lambda v: v.tensor_tensor(C[:, 517:644], C[:, 1:128], C[:, 0:127], ALU.subtract),
               after=[e_cd], sig="VC")
        p.step("vector", lambda v: v.tensor_copy(C[:, 516:517], C[:, 0:1]), sig="VC")
        p.step("vector", lambda v: v.memset(CF[:, 146:147], 1e-5), sig="VC")
        p.step("vector", lambda v: v.memset(CF[:, 147:148], 1e6), sig="VC")
        p.step("vector", lambda v: v.memset(CF[:, 148:149], -0.5 * math.log(128.0)), sig="VC")
        p.step("vector", lambda v: v.memset(C[:, 128:129], 1.0), sig="VC")
        e_ones = p.step("vector", lambda v: v.memset(C[0:1, 129:257], 1.0), sig="VC")
        self.e_ones = e_ones
        p.step("vector", lambda v: v.tensor_scalar(CF[:, 32:40], CF[:, 16:24], -1.0, 1.0, ALU.mult, ALU.add), after=[e_cd], sig="VC")
        p.step("vector", lambda v: v.tensor_scalar(CF[:, 40:48], CF[:, 24:32], -1.0, 1.0, ALU.mult, ALU.add), sig="VC")
        for i, pb in ((0, self.P4), (1, self.P5)):
            e_mm = p.step("tensor", lambda t, i=i, pb=pb: t.matmul(
                pb[:, 0:512], C[0:1, 129:257], self.bf(XS, 1024)[0:1, 512 * i:512 * (i + 1)], start=True, stop=True),
                after=[e_cd, e_ones], sig="TC")
            p.step("vector", lambda v, i=i, pb=pb: v.tensor_copy(
                C2[:, 1056 + 512 * i:1056 + 512 * (i + 1)], pb[:, 0:512]), after=[e_mm], sig="VC")

        RW = lambda i: self.f32(PH + 16384, 64)[0:1, i:i + 1]
        e_pb = p.step("vector", lambda v: v.tensor_copy(C2[:, 2080:2088], CF[:, 136:144]), after=[e_cd], sig="VC")
        e_pq = p.step("vector", lambda v: v.tensor_tensor(C2[:, 2088:2096], C2[:, 2080:2088], C2[:, 2080:2088], ALU.mult), after=[e_pb], sig="VC")
        e_s1 = p.step("tensor", lambda t: t.matmul(self.P6[0:1, 0:8], C[:, 128:129], C2[:, 2080:2088], start=True, stop=True), after=[e_pb, e_ones], sig="TC")
        e_s2 = p.step("tensor", lambda t: t.matmul(self.P6[0:1, 8:16], C[:, 128:129], C2[:, 2088:2096], start=True, stop=True), after=[e_pq], sig="TC")
        e_r1 = p.step("vector", lambda v: v.tensor_reduce(RW(0), self.P6[0:1, 0:8], AX.X, ALU.add), after=[e_s1], sig="VC")
        e_r2 = p.step("vector", lambda v: v.tensor_reduce(RW(1), self.P6[0:1, 8:16], AX.X, ALU.add), after=[e_s2], sig="VC")
        e_m = p.step("vector", lambda v: v.tensor_scalar(RW(2), RW(0), 1.0 / 1024.0, None, ALU.mult), after=[e_r1], sig="VC")
        e_m2 = p.step("vector", lambda v: v.tensor_tensor(RW(3), RW(2), RW(2), ALU.mult), after=[e_m], sig="VC")
        e_va = p.step("vector", lambda v: v.scalar_tensor_tensor(RW(4), RW(1), 1.0 / 1024.0, RW(3), ALU.mult, ALU.subtract), after=[e_m2, e_r2], sig="VC")
        e_ln = p.step("scalar", lambda s: s.activation(RW(5), RW(4), AF.Ln, bias=CF[0:1, 146:147]), after=[e_va], sig="SC")
        e_rs = p.step("scalar", lambda s: s.activation(RW(6), RW(5), AF.Exp, scale=-0.5), after=[e_ln], sig="SC")
        bft = lambda i: C2[0:1, 2104 + i:2105 + i]
        e_c1 = p.step("vector", lambda v: v.tensor_copy(bft(0), RW(2)), after=[e_m], sig="VC")
        e_c2 = p.step("vector", lambda v: v.tensor_copy(bft(1), RW(6)), after=[e_rs], sig="VC")
        e_b1 = p.step("tensor", lambda t: t.matmul(self.P6[:, 20:21], C[0:1, 129:257], bft(0), start=True, stop=True), after=[e_c1], sig="TC")
        e_b2 = p.step("tensor", lambda t: t.matmul(self.P6[:, 21:22], C[0:1, 129:257], bft(1), start=True, stop=True), after=[e_c2], sig="TC")
        e_muc = p.step("vector", lambda v: v.tensor_copy(CF[:, 144:145], self.P6[:, 20:21]), after=[e_b1], sig="VC")
        e_rsc = p.step("vector", lambda v: v.tensor_copy(CF[:, 145:146], self.P6[:, 21:22]), after=[e_b2], sig="VC")
        p.step("vector", lambda v: v.tensor_scalar(
            C2[:, 2096:2104], CF[:, 136:144], CF[:, 144:145], CF[:, 145:146], ALU.subtract, ALU.mult),
            after=[e_muc, e_rsc], sig="VC")
        self.e_utmask = e_cd
        self.p.barrier()

    def ln_tok(self, x_in):
        p, CF = self.p, self.CF
        xt_cols = [PH, PH + 2048]
        xn_cols = [PH + 4096, PH + 5120]
        sq_cols = PH + 6144
        last_xt_reader = [None, None]
        prev_store = [None, None]
        prev_sq = None
        last_stats = [None, None]
        e_store = []
        for j in range(NCH):
            par = j % 2
            xt = self.f32(xt_cols[par], 1024)
            xn = self.bf(xn_cols[par], 1024)
            sb = 88 + 8 * par
            e_ld = p.step("sync", lambda s, j=j, xt=xt: s.dma_start(out=xt, in_=x_in[128 * j:128 * (j + 1), :]),
                          after=[last_xt_reader[par]], sig="SQ", dma=True)
            e_sum = p.step("vector", lambda v, xt=xt, sb=sb: v.reduce_sum(CF[:, sb:sb + 1], xt, axis=AX.X),
                           after=[e_ld, last_stats[par]], sig="VC")
            e_sq = p.step("scalar", lambda s, xt=xt: s.activation(self.f32(sq_cols, 1024), xt, AF.Square),
                          after=[e_ld, prev_sq], sig="SC")
            e_sum2 = p.step("vector", lambda v, sb=sb: v.reduce_sum(CF[:, sb + 1:sb + 2], self.f32(sq_cols, 1024), axis=AX.X),
                            after=[e_sq], sig="VC")
            prev_sq = e_sum2
            e_mu = p.step("vector", lambda v, sb=sb: v.tensor_scalar(CF[:, sb + 2:sb + 3], CF[:, sb:sb + 1], -1.0 / 1024.0, None, ALU.mult),
                          after=[e_sum], sig="VC")
            e_mu2 = p.step("vector", lambda v, sb=sb: v.tensor_tensor(CF[:, sb + 3:sb + 4], CF[:, sb + 2:sb + 3], CF[:, sb + 2:sb + 3], ALU.mult),
                           after=[e_mu], sig="VC")
            e_var = p.step("vector", lambda v, sb=sb: v.scalar_tensor_tensor(
                CF[:, sb + 4:sb + 5], CF[:, sb + 1:sb + 2], 1.0 / 1024.0, CF[:, sb + 3:sb + 4], ALU.mult, ALU.subtract),
                after=[e_mu2, e_sum2], sig="VC")
            e_lnr = p.step("scalar", lambda s, sb=sb: s.activation(CF[:, sb + 5:sb + 6], CF[:, sb + 4:sb + 5], AF.Ln, bias=CF[:, 146:147]),
                           after=[e_var], sig="SC")
            e_rstd = p.step("scalar", lambda s, sb=sb: s.activation(CF[:, sb + 6:sb + 7], CF[:, sb + 5:sb + 6], AF.Exp, scale=-0.5),
                            after=[e_lnr], sig="SC")
            e_xn = p.step("vector", lambda v, xt=xt, xn=xn, sb=sb: v.tensor_scalar(
                xn, xt, CF[:, sb + 2:sb + 3], CF[:, sb + 6:sb + 7], ALU.add, ALU.mult),
                after=[e_rstd, prev_store[par]], sig="VC")
            last_xt_reader[par] = e_xn
            last_stats[par] = e_xn
            ev = p.step("scalar", lambda s, j=j, xn=xn: s.dma_start(out=self.scrA[128 * j:128 * (j + 1), :], in_=xn),
                        after=[e_xn], sig="AQ", dma=True)
            prev_store[par] = ev
            e_store.append(ev)
        e_all = e_store[-1]
        e_tl = []
        for c in range(8):
            e_tl.append(p.step("sync", lambda s, c=c: s.dma_start(
                out=self.bf(XN + 2048 * c, 2048), in_=self.scrA[:, 128 * c:128 * (c + 1)], transpose=True),
                after=[e_all], sig="SQ", dma=True))
        return e_tl

    def v_proj(self, xn_tl, first):
        p = self.p
        e_wv = p.step("gpsimd", lambda g: g.dma_start(
            out=self.bf(QD, 8192).rearrange("p (kc c) -> p kc c", kc=8),
            in_=self.wv.rearrange("(kc p) c -> p kc c", p=128)), sig="LD", dma=True)
        e_ones = None
        if first:
            for j in range(NCH):
                ap = self.A[:, VT + 1056 * j + 64: VT + 1056 * j + 64 + 16 * 66:66]
                e_ones = p.step("gpsimd", lambda g, ap=ap: g.memset(ap, 1.0), sig="PC")
        evs = []
        prev_evac = [None, None]
        for j in range(NCH):
            par = j % 2
            pb = self.PBG[:, 1024 * par:1024 * (par + 1)]
            e_mm = None
            for dh in range(2):
                for kc in range(8):
                    e_mm = p.step("tensor", lambda t, j=j, dh=dh, kc=kc, pb=pb: t.matmul(
                        pb[:, 512 * dh:512 * (dh + 1)],
                        self.bf(XN + 2048 * kc + 128 * j, 128),
                        self.bf(QD + 1024 * kc + 512 * dh, 512),
                        start=(kc == 0), stop=(kc == 7)),
                        after=[e_wv, xn_tl[kc], prev_evac[par]],
                        sig="TC" if (dh == 1 and kc == 7) else None)
            dst = self.A[:, VT + 1056 * j: VT + 1056 * j + 16 * 66].rearrange("p (h c) -> p h c", h=16)[:, :, 0:64]
            src = pb.rearrange("p (h c) -> p h c", h=16)
            ev = p.step("vector", lambda v, dst=dst, src=src: v.tensor_copy(dst, src),
                        after=[e_mm, e_ones], sig="VC")
            prev_evac[par] = ev
            evs.append(ev)
        return evs

    def load_w_slice(self, w_packed, idx, ncols, slot_cols, slot_id, after=()):
        src = w_packed[128 * idx:128 * (idx + 1), :]
        dst = self.bf(slot_cols, 8 * ncols)
        ev = self.p.step("gpsimd", lambda g: g.dma_start(out=dst, in_=src),
                         after=list(after) + [self.ws_user[slot_id]], sig="LD", dma=True)
        return ev

    def proj_pair(self, w_ev, wslot, slot_id, xn_tl, scale, free_evs=None):
        p = self.p
        evs = {}
        for tt in range(4):
            for ci in range(2):
                idx = 2 * tt + ci
                bi = idx % 4
                bank = self.PBG[:, 512 * bi: 512 * (bi + 1)]
                e_mm = None
                for kc in range(8):
                    e_mm = p.step("tensor", lambda t, ci=ci, tt=tt, kc=kc, bank=bank, wslot=wslot: t.matmul(
                        bank, self.bf(wslot + 256 * kc + 128 * ci, 128),
                        self.bf(XN + 2048 * kc + 512 * tt, 512),
                        start=(kc == 0), stop=(kc == 7)),
                        after=([w_ev, xn_tl[kc]] + ([self.pbg_user[bi]] if kc == 0 else [])),
                        sig="TC" if kc == 7 else None)
                e_e = p.step("scalar", lambda s, ci=ci, tt=tt, bank=bank: s.activation(
                    self.bf(QD + 8192 + 2048 * ci + 512 * tt, 512), bank, AF.Copy, scale=scale),
                    after=[e_mm] + ([free_evs[(ci, tt)]] if free_evs else []), sig="SC")
                evs[(ci, tt)] = e_e
                self.pbg_user[bi] = e_e
                self.ws_user[slot_id] = e_mm
        return evs

    def phi_group(self, g, proj_evs, want_M, dst_tok, dstM, dst_free=None):
        p, C, CF = self.p, self.C, self.CF
        e_tok = [None] * NCH
        e_M = [None] * NCH if want_M else None
        for j in range(NCH):
            par = j % 2
            ps = self.P4 if par == 0 else self.P5
            e_mm = None
            for hl in range(4):
                ci, odd = hl // 2, hl % 2
                om0 = 260 if odd == 0 else 644
                src0 = QD + 8192 + 2048 * ci + 128 * j
                e_mm = p.step("tensor", lambda t, hl=hl, ps=ps, om0=om0, src0=src0: t.matmul(
                    ps[:, 128 * hl:128 * (hl + 1)],
                    self.A[:, src0:src0 + 128],
                    C[:, om0:om0 + 128], start=True, stop=True),
                    after=([proj_evs[(0, j // 4)], proj_evs[(1, j // 4)], self.z_sub[par]]
                           if hl == 0 else []),
                    sig="TC" if hl == 3 else None)
            mx0 = 104 + 8 * par
            ps3 = ps[:, 0:512].rearrange("p (h m) -> p h m", h=4)
            e_red = p.step("vector", lambda v, ps3=ps3, mx0=mx0: v.tensor_reduce(
                CF[:, mx0:mx0 + 4], ps3, AX.X, ALU.max), after=[e_mm], sig="VC")
            e_nb = p.step("vector", lambda v, mx0=mx0: v.tensor_scalar(
                CF[:, mx0 + 4:mx0 + 8], CF[:, mx0:mx0 + 4], -1.0, -0.5 * math.log(128.0),
                ALU.mult, ALU.add), after=[e_red], sig="VC")
            zcf = self.f32(QD + 12288 + 2048 * par, 512)
            nb3 = CF[:, mx0 + 4:mx0 + 8][:, :, None].broadcast_to([128, 4, 128])
            e_sub = p.step("vector", lambda v, zcf=zcf, ps3=ps3, nb3=nb3: v.tensor_tensor(
                zcf.rearrange("p (h m) -> p h m", h=4), ps3, nb3, ALU.add),
                after=[e_nb, self.zc_exp[par]], sig="VC")
            self.z_sub[par] = e_sub
            dcol = (dst_tok + 512 * j) if dst_tok is not None else (PH + 16384 + 512 * par)
            dap = self.bf(dcol, 512)
            e_exp = p.step("scalar", lambda s, dap=dap, zcf=zcf: s.activation(dap, zcf, AF.Exp),
                           after=[e_sub, self.scr_T[par] if dst_tok is None else None]
                           + ([dst_free.get(j)] if dst_free else []), sig="SC")
            self.zc_exp[par] = e_exp
            e_t = p.step("vector", lambda v, dap=dap: v.tensor_scalar(dap, dap, FEPS, None, ALU.add),
                         after=[e_exp], sig="VC")
            e_tok[j] = e_t
            if want_M:
                e_T = None
                for hl in range(4):
                    e_T = p.step("tensor", lambda t, hl=hl, par=par, dcol=dcol: t.transpose(
                        self.PT[:, 512 * par + 128 * hl: 512 * par + 128 * (hl + 1)],
                        self.bf(dcol + 128 * hl, 128), C[:, 516:644]),
                        after=([e_t, self.pt_chain] if hl == 0 else []),
                        sig="TC" if hl == 3 else None)
                if dst_tok is None:
                    self.scr_T[par] = e_T
                dM3 = self.A[:, dstM:dstM + 8192].rearrange(
                    "p (hl t) -> p hl t", hl=4)[:, :, 128 * j:128 * (j + 1)]
                pt3 = self.PT[:, 512 * par:512 * (par + 1)].rearrange("p (hl m) -> p hl m", hl=4)
                e_ev = p.step("vector", lambda v, dM3=dM3, pt3=pt3: v.tensor_copy(dM3, pt3),
                              after=[e_T], sig="VC")
                self.pt_chain = e_ev
                e_M[j] = e_ev
        return e_tok, e_M

    def finish_trunc(self):
        p = self.p
        p.barrier()
        ev = p.step("sync", lambda s: s.dma_start(out=self.out[0:128, :], in_=self.f32(XN, 1024)), sig="SQ", dma=True)
        p.step("sync", lambda s: s.nop(), after=[ev])
        return self

    def build(self):
        p = self.p
        self.consts()

        self._stg_st = [None, None]
        self.e_wstage = [None, None]

        def stage_weights(wi):
            wsrc, wdst = ((self.w1p, self.wb1), (self.w2, self.wb2))[wi]
            stg = [XS + 6144, XS + 7168]
            for i in range(32):
                par = i % 2
                e_l = p.step("gpsimd", lambda g, wsrc=wsrc, i=i, par=par, stg=stg: g.dma_start(
                    out=self.bf(stg[par], 1024), in_=wsrc[128 * i:128 * (i + 1), :]),
                    after=[self._stg_st[par]], sig="LD", dma=True)
                self._stg_st[par] = p.step("scalar", lambda s, wdst=wdst, i=i, par=par, stg=stg: s.dma_start(
                    out=wdst[128 * i:128 * (i + 1), :], in_=self.bf(stg[par], 1024)),
                    after=[e_l], sig="AQ", dma=True)
            self.e_wstage[wi] = self._stg_st[(32 - 1) % 2]
        self.stage_weights = stage_weights

        xp_tl = self.ln_tok(self.xp_tok)
        vp_evs = self.v_proj(xp_tl, first=True)
        self.stage_weights(0)
        p.barrier()
        if KTRUNC == 2:
            return self.finish_trunc()
        e_sinit = [None] * 4
        e_sinit_last = None
        for g in range(4):
            e_wk = self.load_w_slice(self.wkp, g, 256, WS + 2048 * (g % 2), g % 2)
            pe = self.proj_pair(e_wk, WS + 2048 * (g % 2), g % 2, xp_tl, SCALE)
            if KTRUNC == 21:
                return self.finish_trunc()
            e_ptok, _ = self.phi_group(g, pe, False, PH + 8192, None,
                                       dst_free={0: e_sinit_last})
            if KTRUNC == 22:
                return self.finish_trunc()
            for hl in range(4):
                e_mm = None
                for j in range(NCH):
                    e_mm = p.step("tensor", lambda t, hl=hl, j=j, g=g: t.matmul(
                        self.P7[:, 65 * hl:65 * hl + 65],
                        self.bf(PH + 8192 + 512 * j + 128 * hl, 128),
                        self.A[:, VT + 1056 * j + 66 * (4 * g + hl): VT + 1056 * j + 66 * (4 * g + hl) + 65],
                        start=(j == 0), stop=(j == NCH - 1)),
                        after=([e_ptok[j], vp_evs[j]] if hl == 0 else [])
                        + ([e_sinit_last] if j == 0 else []),
                        sig="TC" if j == NCH - 1 else None)
                dst = self.C2[:, 66 * (4 * g + hl): 66 * (4 * g + hl) + 65]
                e_sinit[hl] = p.step("vector", lambda v, dst=dst, hl=hl: v.tensor_copy(dst, self.P7[:, 65 * hl:65 * hl + 65]),
                                     after=[e_mm], sig="VC")
                e_sinit_last = e_sinit[hl]
            if KTRUNC == 3:
                return self.finish_trunc()
        e_sim = p.step("vector", lambda v: v.tensor_scalar(
            self.C2[:, 0:1056], self.C2[:, 0:1056], self.CF[:, 80:81], None, ALU.mult),
            after=[e_sinit_last], sig="VC")
        self.tap("sinit", self.C2[:, 0:1056], [e_sim])
        p.barrier()
        if KTRUNC == 4:
            return self.finish_trunc()

        x_tl = self.ln_tok(self.x_tok)
        self.tap("xn", self.bf(XN, 16384), [x_tl[7]])
        v_evs = self.v_proj(x_tl, first=False)
        self.stage_weights(1)
        self.tap("vtok", self.bf(VT, 16896), [v_evs[-1]])
        p.barrier()

        if KTRUNC == 5:
            return self.finish_trunc()
        e_attn_last = None
        for g in range(4):
            e_wk = self.load_w_slice(self.wkp, g, 256, WS + 2048 * (g % 2), g % 2)
            pe = self.proj_pair(e_wk, WS + 2048 * (g % 2), g % 2, x_tl, SCALE)
            e_ktok, e_kM = self.phi_group(g, pe, True, PH + 8192, PH)
            if g == 0:
                self.tap("ktok0", self.bf(PH + 8192, 8192), [e_ktok[15]])
                self.tap("kpM0", self.bf(PH, 8192), [e_kM[15]])
            if KTRUNC == 6:
                return self.finish_trunc()
            e_sr = p.step("vector", lambda v, g=g: v.tensor_copy(
                self.CF[:, 160:424], self.C2[:, 264 * g:264 * g + 264]), sig="VC")
            e_sn = []
            e_add = e_sr
            for j in range(NCH):
                sn0 = SN + 4224 * g
                dst = self.A[:, sn0: sn0 + 4224].rearrange("p (hh c) -> p hh c", hh=4)[:, :, 66 * j:66 * j + 65]
                srun3 = self.CF[:, 160:424].rearrange("p (hh c) -> p hh c", hh=4)[:, :, 0:65]
                e_snap = p.step("vector", lambda v, dst=dst, srun3=srun3: v.tensor_copy(dst, srun3),
                                after=[e_add], sig="VC")
                e_mm = None
                for hl in range(4):
                    e_mm = p.step("tensor", lambda t, hl=hl, j=j, g=g: t.matmul(
                        self.P7[:, 65 * hl:65 * hl + 65],
                        self.bf(PH + 8192 + 512 * j + 128 * hl, 128),
                        self.A[:, VT + 1056 * j + 66 * (4 * g + hl): VT + 1056 * j + 66 * (4 * g + hl) + 65],
                        start=True, stop=True),
                        after=([e_ktok[j], e_add] if hl == 0 else []),
                        sig="TC" if hl == 3 else None)
                ps3 = self.P7[:, 0:260].rearrange("p (hh c) -> p hh c", hh=4)[:, :, 0:65]
                e_add = p.step("vector", lambda v, srun3=srun3, ps3=ps3: v.tensor_tensor(srun3, srun3, ps3, ALU.add),
                               after=[e_mm, e_snap], sig="VC")
                e_sn.append(e_snap)
            e_state_done = e_add
            if g == 0:
                self.tap("snap", self.bf(SN, 4224), [e_state_done])
            if KTRUNC == 7:
                return self.finish_trunc()
            qslot = (g + 1) % 2
            e_wq = self.load_w_slice(self.wqp, g, 256, WS + 2048 * qslot, qslot)
            fq = {(ci, tt): e_ktok[4 * tt + 3] for ci in range(2) for tt in range(4)}
            pq = self.proj_pair(e_wq, WS + 2048 * qslot, qslot, x_tl, SCALE, free_evs=fq)
            e_qtok, e_qM = self.phi_group(g, pq, True, None, PH + 8192)
            if g == 0:
                self.tap("qpM0", self.bf(PH + 8192, 8192), [e_qM[15]])
            if KTRUNC == 8:
                return self.finish_trunc()
            amt = [XS, XS + 512]
            atn = [XS + 1024, XS + 1280]
            prev_eam = None
            prev_amt_user = [None, None]
            prev_atn_user = [None, None]
            prev_numuser = None
            prev_den = None
            for j in range(NCH):
                par = j % 2
                e_at = None
                for hl in range(4):
                    e_at = p.step("tensor", lambda t, hl=hl, j=j: t.matmul(
                        self.P5[:, 128 * hl:128 * (hl + 1)],
                        self.bf(PH + 2048 * hl + 128 * j, 128),
                        self.bf(PH + 8192 + 2048 * hl + 128 * j, 128),
                        start=True, stop=True),
                        after=([e_kM[j], e_qM[j], prev_eam] if hl == 0 else [])
                        + ([self.z_sub[1]] if (j == 0 and hl == 0) else []),
                        sig="TC" if hl == 3 else None)
                ps3 = self.P5[:, 0:512].rearrange("p (hh c) -> p hh c", hh=4)
                mb = self.C[:, 0:128][:, None, :].broadcast_to([128, 4, 128])
                am3 = self.bf(amt[par], 512).rearrange("p (hh c) -> p hh c", hh=4)
                e_am = p.step("vector", lambda v, am3=am3, ps3=ps3, mb=mb: v.tensor_tensor(am3, ps3, mb, ALU.mult),
                              after=[e_at, self.e_utmask, prev_amt_user[par]], sig="VC")
                prev_eam = e_am
                e_nm = None
                for hl in range(4):
                    p.step("tensor", lambda t, hl=hl, j=j, g=g, par=par: t.matmul(
                        self.P6[:, 65 * hl:65 * hl + 65],
                        self.bf(amt[par] + 128 * hl, 128),
                        self.A[:, VT + 1056 * j + 66 * (4 * g + hl): VT + 1056 * j + 66 * (4 * g + hl) + 65],
                        start=True, stop=False),
                        after=[e_am] + ([prev_numuser, e_sn[j]] if hl == 0 else [])
                        + ([e_qM[15]] if (j == 0 and hl == 0) else []), sig=None)
                    e_nm = p.step("tensor", lambda t, hl=hl, j=j, g=g: t.matmul(
                        self.P6[:, 65 * hl:65 * hl + 65],
                        self.bf(PH + 8192 + 2048 * hl + 128 * j, 128),
                        self.A[:, SN + 1056 * (4 * g + hl) + 66 * j: SN + 1056 * (4 * g + hl) + 66 * j + 65],
                        start=False, stop=True),
                        sig="TC" if hl == 3 else None)
                prev_amt_user[par] = e_nm
                den3 = self.P6[:, 0:260].rearrange("p (hh c) -> p hh c", hh=4)[:, :, 64:65]
                if g == 1 and j == 0 and "ps6" in TAPS:
                    t14 = self.nc.dram_tensor("dbg_amtl", [128, 512], BF16, kind="ExternalOutput").ap()
                    p.step("sync", lambda s, t14=t14, par=par: s.dma_start(out=t14, in_=self.bf(amt[par], 512)),
                           after=[e_nm], sig="SQ", dma=True)
                    t15 = self.nc.dram_tensor("dbg_snapl", [128, 260], BF16, kind="ExternalOutput").ap()
                    p.step("sync", lambda s, t15=t15, g=g, j=j: s.dma_start(
                        out=t15, in_=self.A[:, SN + 4224 * g + 66 * j: SN + 4224 * g + 66 * j + 260]),
                        after=[e_nm], sig="SQ", dma=True)
                    t16b = self.nc.dram_tensor("dbg_qpml", [128, 128], BF16, kind="ExternalOutput").ap()
                    p.step("sync", lambda s, t16b=t16b, j=j: s.dma_start(out=t16b, in_=self.bf(PH + 8192 + 128 * j, 128)),
                           after=[e_nm], sig="SQ", dma=True)
                    t12 = self.nc.dram_tensor("dbg_ps6", [128, 260], F32, kind="ExternalOutput").ap()
                    e_cp6 = p.step("vector", lambda v: v.tensor_copy(self.f32(XS + 4096, 260), self.P6[:, 0:260]),
                                   after=[e_nm], sig="VC")
                    p.step("sync", lambda s, t12=t12: s.dma_start(out=t12, in_=self.f32(XS + 4096, 260)),
                           after=[e_cp6], sig="SQ", dma=True)
                e_d1 = p.step("vector", lambda v, den3=den3: v.tensor_scalar(self.CF[:, 424:428], den3, FEPS, None, ALU.add),
                              after=[e_nm, prev_den], sig="VC")
                e_d2 = p.step("vector", lambda v: v.reciprocal(self.CF[:, 428:432], self.CF[:, 424:428]), after=[e_d1], sig="VC")
                num3 = self.P6[:, 0:260].rearrange("p (hh c) -> p hh c", hh=4)[:, :, 0:64]
                rb3 = self.CF[:, 428:432][:, :, None].broadcast_to([128, 4, 64])
                at3 = self.bf(atn[par], 256).rearrange("p (hh c) -> p hh c", hh=4)
                e_at2 = p.step("vector", lambda v, at3=at3, num3=num3, rb3=rb3: v.tensor_tensor(at3, num3, rb3, ALU.mult),
                               after=[e_d2, prev_atn_user[par]], sig="VC")
                if g == 1 and j == 0 and "ps6" in TAPS:
                    t13 = self.nc.dram_tensor("dbg_rden", [128, 16], F32, kind="ExternalOutput").ap()
                    p.step("sync", lambda s, t13=t13: s.dma_start(out=t13, in_=self.CF[:, 424:440]),
                           after=[e_at2], sig="SQ", dma=True)
                prev_numuser = e_at2
                prev_den = e_at2
                e_flush = p.step("vector", lambda v: v.memset(self.CF[:, 439:440], 0.0),
                                 after=[e_at2], sig="VC")
                ev = p.step("scalar", lambda s, j=j, g=g, par=par: s.dma_start(
                    out=self.scrA[128 * j:128 * (j + 1), 256 * g:256 * (g + 1)], in_=self.bf(atn[par], 256)),
                    after=[e_flush], sig="AQ", dma=True)
                if g == 1 and j == 0 and "sc10" in TAPS:
                    t10 = self.nc.dram_tensor("dbg_sc10", [128, 256], BF16, kind="ExternalOutput").ap()
                    t11 = self.nc.dram_tensor("dbg_sb10", [128, 256], BF16, kind="ExternalOutput").ap()
                    p.step("sync", lambda s, t10=t10: s.dma_start(out=t10, in_=self.scrA[0:128, 256:512]),
                           after=[ev], sig="SQ", dma=True)
                    p.step("sync", lambda s, t11=t11, par=par: s.dma_start(out=t11, in_=self.bf(atn[par], 256)),
                           after=[ev], sig="SQ", dma=True)
                prev_atn_user[par] = ev
                e_attn_last = ev
            self.z_sub[1] = prev_eam
            self.pt_chain = prev_numuser
            p.barrier()
            if KTRUNC == 9:
                return self.finish_trunc()

        if KTRUNC == 10:
            return self.finish_trunc()
        e_afm = []
        for c in range(8):
            e_afm.append(p.step("sync", lambda s, c=c: s.dma_start(
                out=self.bf(QD + 2048 * c, 2048), in_=self.scrA[:, 128 * c:128 * (c + 1)], transpose=True),
                after=[e_attn_last], sig="SQ", dma=True))
        if "scra" in TAPS:
            t = self.nc.dram_tensor("dbg_scra", [T, D], BF16, kind="ExternalOutput").ap()
            self.p.step("sync", lambda s, t=t: s.dma_start(out=t, in_=self.scrA), after=[e_attn_last], sig="SQ", dma=True)
            self.p.barrier()
        self.tap("attnfm", self.bf(QD, 16384), [e_afm[7]])
        e_y = {}
        for c in range(8):
            slot = c % 2
            e_w = self.load_w_slice(self.wop, c, 128, WS + 2048 * slot, slot)
            last_mm = None
            for tt in range(4):
                bank = self.PBG[:, 512 * tt:512 * (tt + 1)]
                e_mm = None
                for kc in range(8):
                    e_mm = p.step("tensor", lambda t, c=c, tt=tt, kc=kc, bank=bank, slot=slot: t.matmul(
                        bank, self.bf(WS + 2048 * slot + 128 * kc, 128),
                        self.bf(QD + 2048 * kc + 512 * tt, 512),
                        start=(kc == 0), stop=(kc == 7)),
                        after=[e_w, e_afm[kc]] + ([e_y.get((c - 1, tt))] if c >= 1 else []),
                        sig="TC" if kc == 7 else None)
                e_y[(c, tt)] = p.step("vector", lambda v, c=c, tt=tt, bank=bank: v.tensor_tensor(
                    self.bf(PH + 2048 * c + 512 * tt, 512), bank,
                    self.bf(XN + 2048 * c + 512 * tt, 512), ALU.add),
                    after=[e_mm], sig="VC")
                last_mm = e_mm
            self.ws_user[slot] = last_mm
        self.tap("y", self.bf(PH, 16384), [e_y[(7, 3)]])
        p.barrier()
        if KTRUNC == 11:
            return self.finish_trunc()

        muR = self.f32(WS, 2048)[0:1, :]
        s2R = self.f32(XS + 6144, 2048)[0:1, :]
        e_rowmu = []
        e_rows2 = []
        prev_sqv = None
        for tt in range(4):
            e_m1 = None
            e_m2 = None
            for c in range(8):
                e_q = p.step("scalar", lambda s, c=c, tt=tt: s.activation(
                    self.bf(XS + 4096, 512), self.bf(PH + 2048 * c + 512 * tt, 512), AF.Square),
                    after=[e_y[(c, tt)], prev_sqv], sig="SC")
                e_m1 = p.step("tensor", lambda t, c=c, tt=tt: t.matmul(
                    self.P7[0:1, 0:512], self.C[:, 128:129], self.bf(PH + 2048 * c + 512 * tt, 512),
                    start=(c == 0), stop=(c == 7)),
                    after=[e_y[(c, tt)]] + ([e_rowmu[-1]] if (c == 0 and e_rowmu) else []),
                    sig="TC" if c == 7 else None)
                e_m2 = p.step("tensor", lambda t, c=c, tt=tt: t.matmul(
                    self.P6[0:1, 0:512], self.C[:, 128:129], self.bf(XS + 4096, 512),
                    start=(c == 0), stop=(c == 7)),
                    after=[e_q] + ([e_rows2[-1]] if (c == 0 and e_rows2) else []),
                    sig="TC" if c == 7 else None)
                prev_sqv = e_m2
            e_rowmu.append(p.step("vector", lambda v, tt=tt: v.tensor_copy(muR[0:1, 512 * tt:512 * (tt + 1)], self.P7[0:1, 0:512]),
                                  after=[e_m1], sig="VC"))
            e_rows2.append(p.step("vector", lambda v, tt=tt: v.tensor_copy(s2R[0:1, 512 * tt:512 * (tt + 1)], self.P6[0:1, 0:512]),
                                  after=[e_m2], sig="VC"))
        vR = s2R
        e_nm = p.step("vector", lambda v: v.tensor_scalar(muR, muR, -1.0 / 1024.0, None, ALU.mult),
                      after=[e_rowmu[-1]], sig="VC")
        mu2R = self.f32(XS, 2048)[0:1, :]
        e_mu2 = p.step("vector", lambda v: v.tensor_tensor(mu2R, muR, muR, ALU.mult), after=[e_nm], sig="VC")
        e_v = p.step("vector", lambda v: v.scalar_tensor_tensor(vR, s2R, 1.0 / 1024.0, mu2R, ALU.mult, ALU.subtract),
                     after=[e_mu2, e_rows2[-1]], sig="VC")
        e_l = p.step("scalar", lambda s: s.activation(vR, vR, AF.Ln, bias=self.CF[0:1, 147:148]), after=[e_v], sig="SC")
        e_r = p.step("scalar", lambda s: s.activation(self.bf(PH + 16384, 2048)[0:1, :], vR, AF.Exp, scale=-0.5),
                     after=[e_l], sig="SC")
        e_br = p.step("vector", lambda v: v.tensor_tensor(
            self.bf(PH + 18432, 2048)[0:1, :], muR, self.bf(PH + 16384, 2048)[0:1, :], ALU.mult),
            after=[e_r, e_nm], sig="VC")
        e_bc = []
        for (src0, dst0) in ((PH + 16384, XS), (PH + 18432, XS + 2048)):
            for tt in range(4):
                e_mm = p.step("tensor", lambda t, src0=src0, tt=tt: t.matmul(
                    self.P7[:, 0:512], self.C[0:1, 129:257], self.bf(src0, 2048)[0:1, 512 * tt:512 * (tt + 1)],
                    start=True, stop=True),
                    after=[e_r if src0 == PH + 16384 else e_br, self.e_ones] + ([e_bc[-1]] if e_bc else []),
                    sig="TC")
                e_bc.append(p.step("vector", lambda v, dst0=dst0, tt=tt: v.tensor_copy(
                    self.bf(dst0 + 512 * tt, 512), self.P7[:, 0:512]), after=[e_mm], sig="VC"))
        e_bcl = e_bc[-1]
        e_h2 = []
        for c in range(8):
            e_1 = p.step("vector", lambda v, c=c: v.tensor_tensor(
                self.bf(H2 + 2048 * c, 2048), self.bf(PH + 2048 * c, 2048), self.bf(XS, 2048), ALU.mult),
                after=[e_bcl, e_y[(c, 3)]] + ([e_h2[-1]] if e_h2 else []), sig="VC")
            e_2 = p.step("vector", lambda v, c=c: v.tensor_tensor(
                self.bf(H2 + 2048 * c, 2048), self.bf(H2 + 2048 * c, 2048), self.bf(XS + 2048, 2048), ALU.add),
                after=[e_1], sig="VC")
            e_h2.append(p.step("scalar", lambda s, c=c: s.activation(
                self.bf(H2 + 2048 * c, 2048), self.bf(H2 + 2048 * c, 2048), AF.Identity,
                bias=self.CF[:, 8 + c:9 + c], scale=self.CF[:, 0 + c:1 + c]),
                after=[e_2], sig="SC"))
        self.tap("h2", self.bf(H2, 16384), [e_h2[-1]])
        p.barrier()
        if KTRUNC == 12:
            return self.finish_trunc()

        tmp = XS
        prev_c = None
        e_mix = []
        for (mu0, om0, dst0) in ((16, 32, PH), (24, 40, QD)):
            for c in range(8):
                e_a = p.step("vector", lambda v, c=c, om0=om0: v.tensor_scalar(
                    self.bf(tmp + 1, 2047), self.bf(XN + 2048 * c, 2047), self.CF[:, om0 + c:om0 + c + 1], None, ALU.mult),
                    after=[prev_c], sig="VC")
                e_b = p.step("vector", lambda v, c=c, om0=om0: v.tensor_scalar(
                    self.bf(tmp, 1), self.C2[:, 2096 + c:2097 + c], self.CF[:, om0 + c:om0 + c + 1], None, ALU.mult),
                    after=[e_a], sig="VC")
                e_c = p.step("vector", lambda v, c=c, mu0=mu0, dst0=dst0: v.scalar_tensor_tensor(
                    self.bf(dst0 + 2048 * c, 2048), self.bf(XN + 2048 * c, 2048),
                    self.CF[:, mu0 + c:mu0 + c + 1], self.bf(tmp, 2048), ALU.mult, ALU.add),
                    after=[e_b], sig="VC")
                prev_c = e_c
                e_mix.append(e_c)
        self.tap("xr", self.bf(PH, 16384), [e_mix[-1]])
        self.tap("xk", self.bf(QD, 16384), [e_mix[-1]])
        p.barrier()
        for (wsrc, xsrc, kind) in ((self.wk2p, QD, "k"), (self.wrtp, PH, "r")):
            for c in range(8):
                slot = c % 2
                e_w = self.load_w_slice(wsrc, c, 128, WS + 2048 * slot, slot)
                last_mm = None
                for tt in range(4):
                    bank = self.PBG[:, 512 * tt:512 * (tt + 1)]
                    e_mm = None
                    for kc in range(8):
                        e_mm = p.step("tensor", lambda t, c=c, tt=tt, kc=kc, bank=bank, slot=slot, xsrc=xsrc: t.matmul(
                            bank, self.bf(WS + 2048 * slot + 128 * kc, 128),
                            self.bf(xsrc + 2048 * kc + 512 * tt, 512),
                            start=(kc == 0), stop=(kc == 7)),
                            after=[e_w, e_mix[-1]] + ([self.kk_bank_user.get((c - 1, tt))] if c >= 1 else []),
                            sig="TC" if kc == 7 else None) if True else None
                    if kind == "k":
                        e_rl = p.step("vector", lambda v, c=c, tt=tt, bank=bank: v.tensor_scalar(
                            self.bf(XS + 4096, 512), bank, 0.0, None, ALU.max),
                            after=[e_mm, self.kk_tmp_user], sig="VC")
                        e_kq = p.step("vector", lambda v, c=c, tt=tt: v.tensor_tensor(
                            self.bf(XN + 2048 * c + 512 * tt, 512), self.bf(XS + 4096, 512), self.bf(XS + 4096, 512), ALU.mult),
                            after=[e_rl], sig="VC")
                        self.kk_tmp_user = e_kq
                        self.kk_bank_user[(c, tt)] = e_kq
                    else:
                        e_sg = p.step("scalar", lambda s, c=c, tt=tt, bank=bank: s.activation(
                            self.bf(SN + 2048 * c + 512 * tt, 512), bank, AF.Sigmoid),
                            after=[e_mm], sig="SC")
                        self.kk_bank_user[(c, tt)] = e_sg
                        ev = p.step("scalar", lambda s, c=c, tt=tt: s.dma_start(
                            out=self.rscr[128 * c:128 * (c + 1), 512 * tt:512 * (tt + 1)],
                            in_=self.bf(SN + 2048 * c + 512 * tt, 512)),
                            after=[e_sg], sig="AQ", dma=True)
                        self.r_store = ev
                    last_mm = e_mm
                self.ws_user[slot] = last_mm
            p.barrier()
        self.tap("kact", self.bf(XN, 16384), [self.kk_tmp_user])
        e_rt = []
        for tc in range(NCH):
            e_rt.append(p.step("sync", lambda s, tc=tc: s.dma_start(
                out=self.bf(SN + 1024 * tc, 1024), in_=self.rscr[:, 128 * tc:128 * (tc + 1)], transpose=True),
                after=[self.r_store], sig="SQ", dma=True))
        self.tap("rtok", self.bf(SN, 16384), [e_rt[15]])
        e_wv2t = p.step("gpsimd", lambda g: g.dma_start(
            out=self.bf(PH, 8192).rearrange("p (kc c) -> p kc c", kc=8),
            in_=self.wv2t.rearrange("(kc p) c -> p kc c", p=128)), sig="LD", dma=True)
        p.barrier()
        if KTRUNC == 13:
            return self.finish_trunc()

        w2out = self.f32(XS + 4096, 1024)
        a1 = self.f32(XS + 4096, 1024)
        e_out_final = None
        prev_gbank = [None, None]
        prev_w2mm = {}
        prev_asm = {}
        w1slot = lambda d: WS + 1024 * (d % 4)
        w2slot = lambda d: PH + 8192 + 1024 * (d % 4)
        w1_user = [None] * 4
        w2_user = [None] * 4
        e_wst = self.e_wstage
        for tt in range(4):
            e_gev = {}
            for dff in range(32):
                par = dff % 2
                e_w1 = p.step("scalar", lambda s, dff=dff: s.dma_start(
                    out=self.bf(w1slot(dff), 1024), in_=self.wb1[128 * dff:128 * (dff + 1), :]),
                    after=[w1_user[dff % 4], e_wst[0]], sig="AQ", dma=True)
                bank = self.PBG[:, 1024 * par:1024 * par + 512]
                e_mm = None
                for kc in range(8):
                    e_mm = p.step("tensor", lambda t, dff=dff, kc=kc, tt=tt, bank=bank: t.matmul(
                        bank, self.bf(w1slot(dff) + 128 * kc, 128),
                        self.bf(H2 + 2048 * kc + 512 * tt, 512),
                        start=(kc == 0), stop=(kc == 7)),
                        after=[e_w1, e_h2[kc], prev_gbank[par]] + ([prev_asm.get(tt - 1)] if kc == 0 else []),
                        sig="TC" if kc == 7 else None)
                w1_user[dff % 4] = e_mm
                e_g = p.step("vector", lambda v, dff=dff, bank=bank: v.tensor_scalar(
                    self.bf(QD + 512 * dff, 512), bank, self.CF[:, 48 + dff:49 + dff], 0.0, ALU.add, ALU.max),
                    after=[e_mm], sig="VC")
                prev_gbank[par] = e_g
                e_gev[dff] = e_g
            if KTRUNC == 14 and tt == 0:
                return self.finish_trunc()
            banks = [self.PBG[:, 512 * i:512 * (i + 1)] for i in range(4)] + [self.P4, self.P5, self.P6, self.P7]
            e_w2mm = {}
            for dff in range(32):
                e_w2 = p.step("sync", lambda s, dff=dff: s.dma_start(
                    out=self.bf(w2slot(dff), 1024), in_=self.wb2[128 * dff:128 * (dff + 1), :]),
                    after=[w2_user[dff % 4], e_wst[1]], sig="SQ", dma=True)
                for tl in range(4):
                    for dh in range(2):
                        bk = banks[tl + 4 * dh]
                        e_w2mm[(tl, dh)] = p.step("tensor", lambda t, dff=dff, tl=tl, dh=dh, bk=bk: t.matmul(
                            bk[:, 0:512],
                            self.bf(QD + 512 * dff + 128 * tl, 128),
                            self.bf(w2slot(dff) + 512 * dh, 512),
                            start=(dff == 0), stop=(dff == 31)),
                            after=[e_w2, e_gev[dff]] + ([prev_w2mm.get((tl, dh)), e_gev[31]] if dff == 0 else []),
                            sig="TC" if (tl == 3 and dh == 1) else None)
                w2_user[dff % 4] = e_w2mm[(3, 1)]
            if KTRUNC == 15 and tt == 0:
                return self.finish_trunc()
            if tt == 0 and "gsb2" in TAPS:
                tg1 = self.nc.dram_tensor("dbg_gsb2", [128, 16384], BF16, kind="ExternalOutput").ap()
                p.step("sync", lambda s, tg1=tg1: s.dma_start(out=tg1, in_=self.bf(QD, 16384)),
                       after=[e_w2mm[(3, 1)]], sig="SQ", dma=True)
                tg2 = self.nc.dram_tensor("dbg_w2sl", [128, 1024], BF16, kind="ExternalOutput").ap()
                p.step("sync", lambda s, tg2=tg2: s.dma_start(out=tg2, in_=self.bf(WS + 4096, 1024)),
                       after=[e_w2mm[(3, 1)]], sig="SQ", dma=True)
                e_bc0 = p.step("vector", lambda v: v.tensor_copy(self.f32(XS + 8192, 512), banks[0][:, 0:512]),
                               after=[e_w2mm[(3, 1)]], sig="VC")
                tg3 = self.nc.dram_tensor("dbg_bank0", [128, 512], F32, kind="ExternalOutput").ap()
                p.step("sync", lambda s, tg3=tg3: s.dma_start(out=tg3, in_=self.f32(XS + 8192, 512)),
                       after=[e_bc0], sig="SQ", dma=True)
            if tt == 0:
                self.tap("gsb", self.bf(QD, 16384), [e_gev[31]])
            for tl in range(4):
                tc = 4 * tt + tl
                bk0 = banks[tl]
                bk1 = banks[tl + 4]
                e_e0 = p.step("vector", lambda v, bk0=bk0: v.tensor_tensor(
                    w2out[:, 0:512], bk0[:, 0:512], self.C2[:, 1056:1568], ALU.add),
                    after=[e_w2mm[(3, 1)], prev_asm.get(tt)], sig="VC")
                e_e1 = p.step("vector", lambda v, bk1=bk1: v.tensor_tensor(
                    w2out[:, 512:1024], bk1[:, 0:512], self.C2[:, 1568:2080], ALU.add),
                    after=[e_e0], sig="VC")
                if tt == 0 and tl == 0:
                    self.tap("w2o", w2out, [e_e1])
                e_v0 = None
                e_v1 = None
                for dh in range(2):
                    bk = bk0 if dh == 0 else bk1
                    e_mmv = None
                    for kc in range(8):
                        e_mmv = p.step("tensor", lambda t, tc=tc, dh=dh, kc=kc, bk=bk: t.matmul(
                            bk[:, 0:512],
                            self.bf(XN + 2048 * kc + 128 * tc, 128),
                            self.bf(PH + 1024 * kc + 512 * dh, 512),
                            start=(kc == 0), stop=(kc == 7)),
                            after=[e_wv2t, e_e1 if dh == 0 else None, e_e1 if dh == 1 else None],
                            sig="TC" if kc == 7 else None)
                    if dh == 0:
                        e_v0 = e_mmv
                    else:
                        e_v1 = e_mmv
                xio = self.f32(XS + 6144 + 2048 * (tl % 2), 1024)
                e_x = p.step("sync", lambda s, tc=tc, xio=xio: s.dma_start(
                    out=xio, in_=self.x_tok[128 * tc:128 * (tc + 1), :]),
                    after=[prev_asm.get((tt, tl % 2))], sig="SQ", dma=True)
                e_p1 = p.step("gpsimd", lambda g, xio=xio: g.tensor_tensor(xio, xio, w2out, ALU.add),
                              after=[e_x, e_e1], sig="PC")
                e_a0 = p.step("vector", lambda v, bk0=bk0, tc=tc: v.tensor_tensor(
                    a1[:, 0:512], bk0[:, 0:512], self.bf(SN + 1024 * tc, 512), ALU.mult),
                    after=[e_v0, e_rt[tc], e_e1, e_p1], sig="VC")
                e_a1 = p.step("vector", lambda v, bk1=bk1, tc=tc: v.tensor_tensor(
                    a1[:, 512:1024], bk1[:, 0:512], self.bf(SN + 1024 * tc + 512, 512), ALU.mult),
                    after=[e_v1, e_a0], sig="VC")
                if tt == 0 and tl == 0:
                    self.tap("a1t", a1, [e_a1])
                e_p2 = p.step("vector", lambda v, xio=xio: v.tensor_tensor(xio, xio, a1, ALU.add),
                              after=[e_p1, e_a1], sig="VC")
                e_st = p.step("sync", lambda s, tc=tc, xio=xio: s.dma_start(
                    out=self.out[128 * tc:128 * (tc + 1), :], in_=xio),
                    after=[e_p2], sig="SQ", dma=True)
                prev_asm[(tt, tl % 2)] = e_st
                prev_asm[tt] = e_a1
                prev_w2mm[(tl, 0)] = e_a0
                prev_w2mm[(tl, 1)] = e_a1
                e_out_final = e_st
            p.barrier()
            if KTRUNC == 16 and tt == 0:
                return self.finish_trunc()
            if KTRUNC == 17 and tt == 1:
                return self.finish_trunc()
            if KTRUNC == 18 and tt == 2:
                return self.finish_trunc()
        if KTRUNC == 19:
            return self.finish_trunc()
        p.barrier()
        ev = p.step("sync", lambda s: s.dma_start(
            out=self.out[1920:2048, :], in_=self.f32(XS + 8192, 1024)), sig="SQ", dma=True)
        p.step("sync", lambda s: s.nop(), after=[ev])
        return self


def build_program():
    b = B()
    b.kk_bank_user = {}
    b.kk_tmp_user = None
    b.r_store = None
    b.build()
    nc = b.nc
    sems = {}
    import contextlib
    stack = contextlib.ExitStack()
    for name in b.p.count:
        sems[name] = stack.enter_context(nc.semaphore(name))
    with nc.Block() as block:
        block.tensor(lambda t: b.p.emit_engine("tensor", t, sems))
        block.vector(lambda v: b.p.emit_engine("vector", v, sems))
        block.scalar(lambda s: b.p.emit_engine("scalar", s, sems))
        block.gpsimd(lambda g: b.p.emit_engine("gpsimd", g, sems))
        block.sync(lambda s: b.p.emit_engine("sync", s, sems))
    stack.close()
    return nc


_NC_CACHE = None
_LAST_RES = None


def kernel(**inputs):
    global _NC_CACHE
    x = np.asarray(inputs["L0_attn_batch"])
    wq = np.asarray(inputs["L0_attn_Wq"])
    wk = np.asarray(inputs["L0_attn_Wk"])
    wv = np.asarray(inputs["L0_attn_Wv"])
    wo = np.asarray(inputs["L0_attn_Wo"])
    omega = np.asarray(inputs["L0_attn_omega"])
    w1 = np.asarray(inputs["L0_attn_W1"])
    b1 = np.asarray(inputs["L0_attn_b1"])
    w2 = np.asarray(inputs["L0_attn_W2"])
    b2 = np.asarray(inputs["L0_attn_b2"])
    gamma = np.asarray(inputs["L0_attn_gamma"])
    beta = np.asarray(inputs["L0_attn_beta"])
    wr = np.asarray(inputs["L0_ffn_W_r"])
    wk2 = np.asarray(inputs["L0_ffn_W_k"])
    wv2 = np.asarray(inputs["L0_ffn_W_v"])
    mur = np.asarray(inputs["L0_ffn_mu_r"])
    muk = np.asarray(inputs["L0_ffn_mu_k"])

    def pack_lhst(w, ncols):
        nsl = w.shape[1] // ncols
        a = w.reshape(8, 128, nsl, ncols)
        a = np.transpose(a, (2, 1, 0, 3))
        return np.ascontiguousarray(a.reshape(nsl * 128, 8 * ncols))

    com = {
        "wqp": pack_lhst(wq, 256), "wkp": pack_lhst(wk, 256), "wv": wv,
        "wop": pack_lhst(wo, 128),
        "womega_t": np.ascontiguousarray(omega.T),
        "w1p": pack_lhst(w1, 128), "b1_t": np.ascontiguousarray(b1.reshape(32, 128).T),
        "w2": w2, "b2row": b2.reshape(1, 1024),
        "gamma_t": np.ascontiguousarray(gamma.reshape(8, 128).T),
        "beta_t": np.ascontiguousarray(beta.reshape(8, 128).T),
        "wrtp": pack_lhst(np.ascontiguousarray(wr.T), 128),
        "wk2p": pack_lhst(np.ascontiguousarray(wk2.T), 128),
        "wv2t": np.ascontiguousarray(wv2.T),
        "mur_t": np.ascontiguousarray(mur.reshape(8, 128).T),
        "muk_t": np.ascontiguousarray(muk.reshape(8, 128).T),
        "utmask": np.triu(np.ones((128, 128), np.float32)),
    }
    com = {k: np.ascontiguousarray(v, dtype=np.float32) for k, v in com.items()}
    zrow = np.zeros((128, 8), np.float32)
    zpre = np.zeros((2048, 1024), np.float32)
    in_maps = []
    for core in range(8):
        bb, half = core // 2, core % 2
        m = dict(com)
        m["x_tok"] = np.ascontiguousarray(x[bb, 2048 * half:2048 * (half + 1)])
        m["xp_tok"] = np.ascontiguousarray(x[bb, 0:2048]) if half == 1 else zpre
        m["prevrow_fm"] = (np.ascontiguousarray(x[bb, 2047].reshape(8, 128).T)
                           if half == 1 else zrow)
        m["smask"] = np.full((128, 1), float(half), np.float32)
        in_maps.append(m)

    if _NC_CACHE is None:
        _NC_CACHE = build_program()
    res = run_bass_kernel_spmd(_NC_CACHE, in_maps, core_ids=list(range(8)))
    global _LAST_RES
    _LAST_RES = res
    out = np.empty((4, 4096, 1024), np.float32)
    for core in range(8):
        bb, half = core // 2, core % 2
        out[bb, 2048 * half:2048 * (half + 1)] = res.results[core]["out"]
    return out
